# version 11
# baseline (speedup 1.0000x reference)
import math
import numpy as np
import concourse.bass as bass
import concourse.mybir as mybir
from concourse.bass_utils import run_bass_kernel_spmd

F32 = mybir.dt.float32
BF16 = mybir.dt.bfloat16
AF = mybir.ActivationFunctionType
ALU = mybir.AluOpType
AX = mybir.AxisListType

B, T, D = 4, 4096, 1024
NC = 8
NTG = T // 512
PI = float(np.pi)
EXPB_B0 = 20.0
EXPB_A1 = 50.0

ENGINES = ("sync", "tensor", "vector", "scalar", "gpsimd")


class Buf:
    __slots__ = ("w", "readers", "name")

    def __init__(self, name=""):
        self.w = None
        self.readers = []
        self.name = name

    def read_deps(self):
        return [self.w] if self.w is not None else []

    def write_deps(self):
        deps = [self.w] if self.w is not None else []
        return deps + self.readers


class Prog:
    def __init__(self, nc):
        self.nc = nc
        self.ops = {e: [] for e in ENGINES}
        self.engine_sem = {"tensor": "tC", "vector": "vC", "scalar": "sC",
                          "gpsimd": "gC"}
        self.dma_sem = {"sync": "syD", "gpsimd": "gD", "scalar": "scD"}
        self.sem_names = ["tC", "vC", "sC", "gC", "syD", "gD", "scD"]
        self.plan = {s: 0 for s in self.sem_names}
        self.n_ops = 0

    def cur_tokens(self):
        toks = []
        for e in ("tensor", "vector", "scalar", "gpsimd"):
            s = self.engine_sem[e]
            if self.plan[s]:
                toks.append((s, self.plan[s], e, False))
        for e, s in self.dma_sem.items():
            if self.plan[s]:
                toks.append((s, self.plan[s], e, True))
        return toks

    def op(self, engine, fn, reads=(), writes=(), after=(), dma=False):
        assert engine in ENGINES
        if dma:
            sem = self.dma_sem[engine]
            inc = 16
        else:
            sem = self.engine_sem[engine]
            inc = 1
        deps = set(after)
        for b in reads:
            deps.update(b.read_deps())
        for b in writes:
            deps.update(b.write_deps())
        waits = {}
        for (s, v, peng, pdma) in deps:
            assert v <= self.plan[s], "dependency on future op"
            if peng == engine and not pdma:
                continue
            waits[s] = max(waits.get(s, 0), v)
        if dma and self.plan[sem]:
            waits[sem] = max(waits.get(sem, 0), self.plan[sem])
        wl = [(s, v) for (s, v) in waits.items()]
        self.plan[sem] += inc
        token = (sem, self.plan[sem], engine, dma)
        for b in reads:
            b.readers.append(token)
        for b in writes:
            b.w = token
            b.readers = []
        self.ops[engine].append((wl, fn, sem, inc))
        self.n_ops += 1
        return token

    def barrier(self):
        return self.cur_tokens()

    def make_emitter(self, eng_name, sems):
        oplist = self.ops[eng_name]

        def emitter(eng):
            hwm = {}
            for (wl, fn, semname, inc) in oplist:
                for (s, v) in wl:
                    if hwm.get(s, 0) >= v:
                        continue
                    eng.wait_ge(sems[s], v)
                    hwm[s] = v
                last = fn(eng)
                assert last is not None, f"op on {eng_name} returned None"
                last.then_inc(sems[semname], inc)
        return emitter


def build_and_finalize(nc, prog):
    import contextlib
    with contextlib.ExitStack() as st:
        sems = {}
        for s in prog.sem_names:
            sems[s] = st.enter_context(nc.semaphore(s))
        blk = st.enter_context(nc.Block())
        blk.sync(prog.make_emitter("sync", sems))
        blk.tensor(prog.make_emitter("tensor", sems))
        blk.vector(prog.make_emitter("vector", sems))
        blk.scalar(prog.make_emitter("scalar", sems))
        blk.gpsimd(prog.make_emitter("gpsimd", sems))


class Tile:

    def __init__(self, slab, dtype, word_off, cols):
        self.slab = slab
        self.dtype = dtype
        self.word_off = word_off
        self.cols = cols

    def ap(self):
        a = self.slab.ap()
        if self.dtype == F32:
            return a[:, self.word_off:self.word_off + self.cols]
        a = a.bitcast(BF16)
        return a[:, 2 * self.word_off:2 * self.word_off + self.cols]


class Ctx:
    def __init__(self, nc, prog):
        self.nc = nc
        self.P = prog
        self.sb = {}
        self.dram = {}
        self.ps = []
        self.scratch = []

    def tile(self, name, slab, dtype, word_off, cols, scratch=True):
        self.sb[name] = (Tile(slab, dtype, word_off, cols), Buf(name))
        if scratch:
            self.scratch.append(name)

    def direct(self, name, handle):
        self.sb[name] = (handle, Buf(name))

    def get(self, name):
        return self.sb[name]

    def renew(self, fence_toks, *names):
        for n in names:
            h, _ = self.sb[n]
            b = Buf(n)
            b.readers = list(fence_toks)
            self.sb[n] = (h, b)



def emit_ln_tg(ctx, src_name, dst_name):
    nc, P = ctx.nc, ctx.P
    src, src_b = ctx.get(src_name)
    dst, dst_b = ctx.get(dst_name)
    x2, x2_b = ctx.get("lnx2")
    xb, xb_b = ctx.get("lnxb")
    ceps, ceps_b = ctx.get("ceps")
    oc64, _ = ctx.get("ones_c64")
    orow, _ = ctx.get("ones_row")
    tmp, tmp_b = ctx.get("lntmp")
    lnv, lnv_b = ctx.get("lnv2")
    psA, psA_b = ctx.ps[6]
    psB, psB_b = ctx.ps[7]

    def ch(t, c):
        return t.ap().rearrange("p (c t) -> p c t", c=NC)[:, c, :]

    P.op("scalar", lambda e: e.activation(
        out=x2.ap().rearrange("p (c t) -> p c t", c=NC),
        in_=src.ap().rearrange("p (c t) -> p c t", c=NC), func=AF.Square),
        reads=[src_b], writes=[x2_b])
    P.op("gpsimd", lambda e: e.tensor_copy(
        xb.ap().rearrange("p (c t) -> p c t", c=NC),
        src.ap().rearrange("p (c t) -> p c t", c=NC)),
        reads=[src_b], writes=[xb_b])

    def f_sums(eng):
        last = None
        for c in range(NC):
            last = eng.matmul(psA.ap()[0:64, :], oc64.ap(), ch(xb, c),
                              start=(c == 0), stop=(c == NC - 1))
        for c in range(NC):
            last = eng.matmul(psB.ap()[0:64, :], oc64.ap(), ch(x2, c),
                              start=(c == 0), stop=(c == NC - 1))
        return last
    P.op("tensor", f_sums, reads=[xb_b, x2_b], writes=[psA_b, psB_b])

    def f_stats(eng):
        eng.tensor_scalar_mul(tmp.ap()[0:64, :], psA.ap()[0:64, :], 1.0 / D)
        eng.tensor_scalar_mul(lnv.ap()[0:64, :], psB.ap()[0:64, :], 1.0 / D)
        eng.tensor_tensor(psA.ap()[0:64, :], tmp.ap()[0:64, :], tmp.ap()[0:64, :],
                          ALU.mult)
        return eng.tensor_sub(lnv.ap()[0:64, :], lnv.ap()[0:64, :], psA.ap()[0:64, :])
    P.op("vector", f_stats, reads=[psA_b, psB_b], writes=[tmp_b, lnv_b, psA_b])

    def f_rstd(eng):
        eng.activation(out=lnv.ap()[0:64, :], in_=lnv.ap()[0:64, :], func=AF.Ln,
                       bias=ceps.ap()[0:64, 0:1])
        return eng.activation(out=lnv.ap()[0:64, :], in_=lnv.ap()[0:64, :],
                              func=AF.Exp, scale=-0.5)
    P.op("scalar", f_rstd, reads=[lnv_b, ceps_b], writes=[lnv_b])

    P.op("vector", lambda e: e.scalar_tensor_tensor(
        tmp.ap()[0:64, :], tmp.ap()[0:64, :], -1.0, lnv.ap()[0:64, :],
        ALU.mult, ALU.mult), reads=[tmp_b, lnv_b], writes=[tmp_b])

    def f_bc(eng):
        eng.matmul(psB.ap()[:, :], orow.ap(), lnv.ap()[0:1, :], start=True, stop=True)
        return eng.matmul(psA.ap()[:, :], orow.ap(), tmp.ap()[0:1, :], start=True, stop=True)
    P.op("tensor", f_bc, reads=[tmp_b, lnv_b], writes=[psA_b, psB_b])

    for c in range(NC):
        def f_m(eng, c=c):
            return eng.tensor_tensor(tmp.ap(), ch(src, c), psB.ap(), ALU.mult)
        P.op("vector", f_m, reads=[src_b, psB_b, tmp_b], writes=[tmp_b])

        def f_a(eng, c=c):
            return eng.tensor_tensor(ch(dst, c), tmp.ap(), psA.ap(), ALU.add)
        P.op("vector", f_a, reads=[tmp_b, psA_b], writes=[dst_b])



def emit_load_res(ctx, src_dram, tg, dst_name):
    P = ctx.P
    dst, dst_b = ctx.get(dst_name)

    def f(e):
        return e.dma_start(
            out=dst.ap().rearrange("p (c t) -> p c t", c=NC),
            in_=src_dram.ap().rearrange("(c p) t -> p c t", p=128)[:, :, tg * 512:(tg + 1) * 512])
    P.op("sync", f, writes=[dst_b], dma=True)


def emit_store_res(ctx, dst_dram, tg, src_name):
    P = ctx.P
    src, src_b = ctx.get(src_name)

    def f(e):
        return e.dma_start(
            out=dst_dram.ap().rearrange("(c p) t -> p c t", p=128)[:, :, tg * 512:(tg + 1) * 512],
            in_=src.ap().rearrange("p (c t) -> p c t", c=NC))
    return P.op("gpsimd", f, reads=[src_b], dma=True)


def emit_load_w(ctx, dram_name, dst_name):
    P = ctx.P
    dst, dst_b = ctx.get(dst_name)
    dh = ctx.dram[dram_name]
    nchunks = dh.ap().shape[0]

    def f(e):
        return e.dma_start(out=dst.ap().rearrange("p (c d) -> p c d", c=nchunks),
                           in_=dh.ap().transpose([1, 0, 2]))
    P.op("sync", f, writes=[dst_b], dma=True)


def emit_proj_tg(ctx, src_name, dst_name, w_name, *, act=None, act_scale=1.0,
                 act_bias_name=None, psum_banks=(0, 1), dst_off=0):
    nc, P = ctx.nc, ctx.P
    src, src_b = ctx.get(src_name)
    dst, dst_b = ctx.get(dst_name)
    w, w_b = ctx.get(w_name)
    if act_bias_name is not None:
        bias, bias_b = ctx.get(act_bias_name)
    for do in range(NC):
        psh, psb = ctx.ps[psum_banks[do % len(psum_banks)]]

        def f_mm(eng, do=do, psh=psh):
            last = None
            for dc in range(NC):
                lhsT = w.ap()[:, dc * D + do * 128: dc * D + do * 128 + 128]
                rhs = src.ap().rearrange("p (c t) -> p c t", c=NC)[:, dc, :]
                last = eng.matmul(psh.ap(), lhsT, rhs,
                                  start=(dc == 0), stop=(dc == NC - 1))
            return last
        P.op("tensor", f_mm, reads=[src_b, w_b], writes=[psb])

        o_ap = dst.ap()[:, dst_off + do * 512: dst_off + do * 512 + 512]

        def f_ev(eng, do=do, psh=psh, o_ap=o_ap):
            kw = {}
            if act_bias_name is not None:
                kw["bias"] = bias.ap()[:, do:do + 1]
            return eng.activation(out=o_ap, in_=psh.ap(),
                                  func=(act or AF.Copy), scale=act_scale, **kw)
        reads = [psb] + ([bias_b] if act_bias_name is not None else [])
        P.op("scalar", f_ev, reads=reads, writes=[dst_b])



def emit_ffn0(ctx, src_dram, dst_dram):
    nc, P = ctx.nc, ctx.P
    hbuf, hbuf_b = ctx.get("hfull")

    emit_load_w(ctx, "w_ffn0_w", "wslot0")
    emit_load_w(ctx, "w_ffn0_a1", "wslot1")
    for tg in range(NTG):
        rb = f"rbuf{tg % 2}"
        emit_load_res(ctx, src_dram, tg, rb)
        emit_ln_tg(ctx, rb, "xg")
        emit_proj_tg(ctx, "xg", "hfull", "wslot0", dst_off=tg * 512 * NC)
    for tg in range(NTG):
        xg, xg_b = ctx.get("xg")

        def f_cp(eng, tg=tg):
            return eng.tensor_copy(
                xg.ap().rearrange("p (c t) -> p c t", c=NC),
                hbuf.ap()[:, tg * 512 * NC:(tg + 1) * 512 * NC].rearrange("p (c t) -> p c t", c=NC))
        P.op("vector", f_cp, reads=[hbuf_b], writes=[xg_b])
        emit_proj_tg(ctx, "xg", "hfull", "wslot1", dst_off=tg * 512 * NC)
    emit_load_w(ctx, "w_ffn0_a2", "wslot0")
    for tg in range(NTG):
        rb = f"rbuf{tg % 2}"
        emit_load_res(ctx, src_dram, tg, rb)
        xg, xg_b = ctx.get("xg")

        def f_cp(eng, tg=tg):
            return eng.tensor_copy(
                xg.ap().rearrange("p (c t) -> p c t", c=NC),
                hbuf.ap()[:, tg * 512 * NC:(tg + 1) * 512 * NC].rearrange("p (c t) -> p c t", c=NC))
        P.op("vector", f_cp, reads=[hbuf_b], writes=[xg_b])
        emit_proj_tg(ctx, "xg", "tg_out", "wslot0", act=AF.Tanh,
                     act_bias_name="b_ffn0")
        emit_add_store(ctx, dst_dram, tg, rb, "tg_out")


def emit_add_store(ctx, dst_dram, tg, rb_name, add_name):
    P = ctx.P
    rbuf, rbuf_b = ctx.get(rb_name)
    addt, addt_b = ctx.get(add_name)
    for c in range(NC):
        def f(eng, c=c):
            o = rbuf.ap().rearrange("p (c t) -> p c t", c=NC)[:, c, :]
            i = addt.ap().rearrange("p (c t) -> p c t", c=NC)[:, c, :]
            return eng.tensor_add(o, o, i)
        P.op("vector", f, reads=[addt_b, rbuf_b], writes=[rbuf_b])
    emit_store_res(ctx, dst_dram, tg, rb_name)



def emit_ffn1(ctx, src_dram, dst_dram):
    nc, P = ctx.nc, ctx.P
    k2, k2_b = ctx.get("hfull")
    mur, mur_b = ctx.get("v_mu_r")
    muk, muk_b = ctx.get("v_mu_k")
    carry, carry_b = ctx.get("carry8")
    tmp, tmp_b = ctx.get("lntmp")

    def mix_tg(tg, mu_t, mu_b, out_name):
        xg, xg_b = ctx.get("xg")
        xp, xp_b = ctx.get("xp")
        out, out_b = ctx.get(out_name)
        for c in range(NC):
            def f_shift(eng, c=c, tg=tg):
                o = xp.ap().rearrange("p (c t) -> p c t", c=NC)
                i = xg.ap().rearrange("p (c t) -> p c t", c=NC)
                eng.tensor_copy(o[:, c, 1:512], i[:, c, 0:511])
                if tg == 0:
                    return eng.memset(o[:, c, 0:1], 0.0)
                return eng.tensor_copy(o[:, c, 0:1], carry.ap()[:, c:c + 1])
            P.op("vector", f_shift, reads=[xg_b, carry_b], writes=[xp_b])
        def f_carry(eng, tg=tg):
            last = None
            for c in range(NC):
                i = xg.ap().rearrange("p (c t) -> p c t", c=NC)[:, c, 511:512]
                last = eng.tensor_copy(carry.ap()[:, c:c + 1], i)
            return last
        P.op("gpsimd", f_carry, reads=[xg_b], writes=[carry_b])
        for c in range(NC):
            def f_d(eng, c=c):
                return eng.tensor_sub(
                    tmp.ap(),
                    xg.ap().rearrange("p (c t) -> p c t", c=NC)[:, c, :],
                    xp.ap().rearrange("p (c t) -> p c t", c=NC)[:, c, :])
            P.op("gpsimd", f_d, reads=[xg_b, xp_b], writes=[tmp_b])

            def f_mx(eng, c=c):
                return eng.scalar_tensor_tensor(
                    out.ap().rearrange("p (c t) -> p c t", c=NC)[:, c, :],
                    tmp.ap(), mu_t.ap()[:, c:c + 1],
                    xp.ap().rearrange("p (c t) -> p c t", c=NC)[:, c, :],
                    ALU.mult, ALU.add)
            P.op("vector", f_mx, reads=[tmp_b, xp_b, mu_b], writes=[out_b])

    emit_load_w(ctx, "w_ffn1_k", "wslot0")
    for tg in range(NTG):
        rb = f"rbuf{tg % 2}"
        emit_load_res(ctx, src_dram, tg, rb)
        emit_ln_tg(ctx, rb, "xg")
        mix_tg(tg, muk, muk_b, "mixg")
        mixg, mixg_b = ctx.get("mixg")
        for do in range(NC):
            psh, psb = ctx.ps[do % 2]

            def f_mm(eng, do=do, psh=psh):
                last = None
                for dc in range(NC):
                    w, w_b = ctx.get("wslot0")
                    lhsT = w.ap()[:, dc * D + do * 128: dc * D + do * 128 + 128]
                    rhs = mixg.ap().rearrange("p (c t) -> p c t", c=NC)[:, dc, :]
                    last = eng.matmul(psh.ap(), lhsT, rhs,
                                      start=(dc == 0), stop=(dc == NC - 1))
                return last
            P.op("tensor", f_mm, reads=[mixg_b, ctx.get("wslot0")[1]], writes=[psb])

            def f_rl(eng, do=do, psh=psh, tg=tg):
                return eng.activation(out=tmp.ap(), in_=psh.ap(), func=AF.Relu)
            P.op("scalar", f_rl, reads=[psb], writes=[tmp_b])

            def f_sq(eng, do=do, tg=tg):
                o = k2.ap()[:, tg * 512 * NC + do * 512: tg * 512 * NC + do * 512 + 512]
                return eng.tensor_tensor(o, tmp.ap(), tmp.ap(), ALU.mult)
            P.op("vector", f_sq, reads=[tmp_b], writes=[k2_b])

    emit_load_w(ctx, "w_ffn1_r", "wslot1")
    emit_load_w(ctx, "w_ffn1_v", "wslot0")
    carry2, carry2_b = ctx.get("carry8")
    P.op("vector", lambda e: e.memset(carry2.ap(), 0.0), writes=[carry2_b])
    for tg in range(NTG):
        rb = f"rbuf{tg % 2}"
        emit_load_res(ctx, src_dram, tg, rb)
        emit_ln_tg(ctx, rb, "xg")
        mix_tg(tg, mur, mur_b, "mixg")
        mixg, mixg_b = ctx.get("mixg")
        rt, rt_b = ctx.get("xp")
        for do in range(NC):
            psh, psb = ctx.ps[do % 2]

            def f_mm(eng, do=do, psh=psh):
                last = None
                for dc in range(NC):
                    w, w_b = ctx.get("wslot1")
                    lhsT = w.ap()[:, dc * D + do * 128: dc * D + do * 128 + 128]
                    rhs = mixg.ap().rearrange("p (c t) -> p c t", c=NC)[:, dc, :]
                    last = eng.matmul(psh.ap(), lhsT, rhs,
                                      start=(dc == 0), stop=(dc == NC - 1))
                return last
            P.op("tensor", f_mm, reads=[mixg_b, ctx.get("wslot1")[1]], writes=[psb])

            def f_sig(eng, do=do, psh=psh):
                o = rt.ap().rearrange("p (c t) -> p c t", c=NC)[:, do, :]
                return eng.activation(out=o, in_=psh.ap(), func=AF.Sigmoid)
            P.op("scalar", f_sig, reads=[psb], writes=[rt_b])
        outg, outg_b = ctx.get("tg_out")
        for do in range(NC):
            psh, psb = ctx.ps[2 + do % 2]

            def f_mm(eng, do=do, psh=psh, tg=tg):
                last = None
                for dc in range(NC):
                    w, w_b = ctx.get("wslot0")
                    lhsT = w.ap()[:, dc * D + do * 128: dc * D + do * 128 + 128]
                    rhs = k2.ap()[:, tg * 512 * NC + dc * 512: tg * 512 * NC + dc * 512 + 512]
                    last = eng.matmul(psh.ap(), lhsT, rhs,
                                      start=(dc == 0), stop=(dc == NC - 1))
                return last
            P.op("tensor", f_mm, reads=[k2_b, ctx.get("wslot0")[1]], writes=[psb])

            def f_mul(eng, do=do, psh=psh):
                o = outg.ap().rearrange("p (c t) -> p c t", c=NC)[:, do, :]
                return eng.tensor_tensor(o, psh.ap(),
                                         rt.ap().rearrange("p (c t) -> p c t", c=NC)[:, do, :],
                                         ALU.mult)
            P.op("vector", f_mul, reads=[psb, rt_b], writes=[outg_b])
        emit_add_store(ctx, dst_dram, tg, rb, "tg_out")



CA0 = 128
A0EPS = 1e-6


def emit_a0_tables(ctx):
    nc, P = ctx.nc, ctx.P
    tab, tab_b = ctx.get("a0_tab")
    mdec, mdec_b = ctx.get("a0_mdec")
    qdecb, qdecb_b = ctx.get("a0_qdecb")
    kdec, kdec_b = ctx.get("a0_kdec")
    ident, ident_b = ctx.get("ident")
    cmask, cmask_b = ctx.get("a0_cmask")
    gam = ctx.dram["a0_gam"]
    gam_sb, gam_b = ctx.get("a0_gamsb")
    ones_row, or_b = ctx.get("ones_row")
    psA, psA_b = ctx.ps[6]
    ALP = nc.allow_low_precision

    P.op("sync", lambda e: e.dma_start(out=gam_sb.ap()[0:1, 0:2], in_=gam.ap()),
         writes=[gam_b], dma=True)
    P.op("sync", lambda e: e.dma_start(out=cmask.ap(), in_=ctx.dram["cmaskTd"].ap()),
         writes=[cmask_b], dma=True)

    P.op("scalar", lambda e: e.activation(out=tab.ap()[0:1, 0:1],
                                          in_=gam_sb.ap()[0:1, 0:1], func=AF.Abs),
         reads=[gam_b], writes=[tab_b])

    def f_lng(eng):
        eng.activation(out=tab.ap()[0:1, 1:2], in_=tab.ap()[0:1, 0:1], func=AF.Ln)
        eng.activation(out=tab.ap()[0:1, 2:3], in_=tab.ap()[0:1, 1:2], func=AF.Exp,
                       scale=float(CA0))
        return eng.sign(tab.ap()[0:1, 3:4], gam_sb.ap()[0:1, 0:1])
    P.op("scalar", f_lng, reads=[tab_b, gam_b], writes=[tab_b])

    P.op("vector", lambda e: e.memset(tab.ap()[0:1, 4:5], 1.0), writes=[tab_b])

    def f_bc_scal(eng):
        return eng.matmul(psA.ap()[:, 0:4], ones_row.ap(), tab.ap()[0:1, 0:4],
                          start=True, stop=True)
    P.op("tensor", f_bc_scal, reads=[tab_b], writes=[psA_b])

    def f_cp_scal(eng):
        e = eng
        e.copy(tab.ap()[:, 8:9], psA.ap()[:, 1:2])
        e.copy(tab.ap()[:, 9:10], psA.ap()[:, 2:3])
        e.copy(kdec.ap()[:, 1:2], psA.ap()[:, 2:3])
        return e.copy(tab.ap()[:, 10:11], psA.ap()[:, 3:4])
    P.op("scalar", f_cp_scal, reads=[psA_b], writes=[tab_b, kdec_b])

    def f_par(eng):
        return eng.iota(tab.ap()[0:1, 640:768].rearrange("p (a b) -> p a b", a=64),
                        [[0, 64], [1, 2]], base=0, channel_multiplier=0,
                        allow_small_or_imprecise_dtypes=True)
    P.op("gpsimd", f_par, writes=[tab_b])

    def f_sgnm1(eng):
        return eng.activation(out=tab.ap()[0:1, 5:6], in_=tab.ap()[0:1, 3:4],
                              func=AF.Copy, bias=-1.0)
    P.op("scalar", f_sgnm1, reads=[tab_b], writes=[tab_b])

    def f_sgnrows(eng):
        eng.tensor_scalar(tab.ap()[0:1, 768:896], tab.ap()[0:1, 640:768],
                          tab.ap()[0:1, 5:6], 1.0, ALU.mult, ALU.add)
        eng.tensor_scalar(tab.ap()[0:1, 896:1024], tab.ap()[0:1, 640:768],
                          -1.0, 1.0, ALU.mult, ALU.add)
        return eng.tensor_scalar(tab.ap()[0:1, 896:1024], tab.ap()[0:1, 896:1024],
                                 tab.ap()[0:1, 5:6], 1.0, ALU.mult, ALU.add)
    P.op("vector", f_sgnrows, reads=[tab_b], writes=[tab_b])

    def f_sgncol(eng):
        return eng.matmul(psA.ap()[:, 4:5], tab.ap()[0:1, 896:1024],
                          tab.ap()[0:1, 4:5], start=True, stop=True)
    P.op("tensor", f_sgncol, reads=[tab_b], writes=[psA_b])

    P.op("vector", lambda e: e.tensor_copy(tab.ap()[:, 11:12], psA.ap()[:, 4:5]),
         reads=[psA_b], writes=[tab_b])

    def f_mdiota(eng):
        return eng.iota(tab.ap()[:, 256:384], [[1, 128]], base=0,
                        channel_multiplier=-1, allow_small_or_imprecise_dtypes=True)
    P.op("gpsimd", f_mdiota, writes=[tab_b])

    P.op("vector", lambda e: e.tensor_scalar_mul(tab.ap()[:, 256:384],
                                                 tab.ap()[:, 256:384],
                                                 tab.ap()[:, 8:9]),
         reads=[tab_b], writes=[tab_b])
    P.op("scalar", lambda e: e.activation(out=tab.ap()[:, 256:384],
                                          in_=tab.ap()[:, 256:384], func=AF.Exp),
         reads=[tab_b], writes=[tab_b])

    def f_sgnouter(eng):
        return eng.matmul(psA.ap()[:, 128:256], tab.ap()[0:1, 768:896],
                          tab.ap()[0:1, 768:896], start=True, stop=True)
    P.op("tensor", f_sgnouter, reads=[tab_b], writes=[psA_b])

    def f_mdec(eng):
        eng.tensor_tensor(tab.ap()[:, 384:512], tab.ap()[:, 256:384],
                          psA.ap()[:, 128:256], ALU.mult)
        with ALP(reason="bf16 decay mask"):
            return eng.tensor_tensor(mdec.ap(), tab.ap()[:, 384:512], cmask.ap(),
                                     ALU.mult)
    P.op("vector", f_mdec, reads=[tab_b, psA_b, cmask_b], writes=[mdec_b, tab_b])

    def f_kdiota(eng):
        return eng.iota(tab.ap()[:, 12:13], [[0, 1]], base=CA0 - 1,
                        channel_multiplier=-1, allow_small_or_imprecise_dtypes=True)
    P.op("gpsimd", f_kdiota, writes=[tab_b])
    P.op("vector", lambda e: e.tensor_scalar_mul(tab.ap()[:, 12:13],
                                                 tab.ap()[:, 12:13], tab.ap()[:, 8:9]),
         reads=[tab_b], writes=[tab_b])
    P.op("scalar", lambda e: e.activation(out=tab.ap()[:, 13:14],
                                          in_=tab.ap()[:, 12:13], func=AF.Exp),
         reads=[tab_b], writes=[tab_b])
    P.op("vector", lambda e: e.tensor_tensor(tab.ap()[:, 14:15], tab.ap()[:, 13:14],
                                             tab.ap()[:, 11:12], ALU.mult),
         reads=[tab_b], writes=[tab_b])
    P.op("scalar", lambda e: e.copy(kdec.ap()[:, 0:1], tab.ap()[:, 14:15]),
         reads=[tab_b], writes=[kdec_b])

    def f_qiota(eng):
        return eng.iota(tab.ap()[0:1, 128:256], [[1, 128]], base=1,
                        channel_multiplier=0, allow_small_or_imprecise_dtypes=True)
    P.op("gpsimd", f_qiota, writes=[tab_b])
    P.op("vector", lambda e: e.tensor_scalar_mul(tab.ap()[0:1, 128:256],
                                                 tab.ap()[0:1, 128:256],
                                                 tab.ap()[0:1, 1:2]),
         reads=[tab_b], writes=[tab_b])
    P.op("scalar", lambda e: e.activation(out=tab.ap()[0:1, 128:256],
                                          in_=tab.ap()[0:1, 128:256], func=AF.Exp),
         reads=[tab_b], writes=[tab_b])
    P.op("vector", lambda e: e.tensor_tensor(tab.ap()[0:1, 128:256],
                                             tab.ap()[0:1, 128:256],
                                             tab.ap()[0:1, 896:1024], ALU.mult),
         reads=[tab_b], writes=[tab_b])

    def f_qdbc(eng):
        return eng.matmul(psA.ap()[:, 128:256], ones_row.ap(),
                          tab.ap()[0:1, 128:256], start=True, stop=True)
    P.op("tensor", f_qdbc, reads=[tab_b], writes=[psA_b])

    def f_qdcp(eng):
        with ALP(reason="bf16 decay table"):
            return eng.tensor_copy(qdecb.ap(), psA.ap()[:, 128:256])
    P.op("vector", f_qdcp, reads=[psA_b], writes=[qdecb_b])


def emit_a0(ctx, src_dram, dst_dram):
    nc, P = ctx.nc, ctx.P
    grp, grp_b = ctx.get("a0_grp")
    S, S_b = ctx.get("a0_S")
    z, z_b = ctx.get("a0_z")
    zbf, zbf_b = ctx.get("a0_zbf")
    mdec, mdec_b = ctx.get("a0_mdec")
    qdecb, qdecb_b = ctx.get("a0_qdecb")
    kdec, kdec_b = ctx.get("a0_kdec")
    bv_bc, bv_bc_b = ctx.get("a0_bvbc")
    ident, ident_b = ctx.get("ident")
    ktm, ktm_b = ctx.get("a0_ktm")
    bq, bq_b = ctx.get("a0_bqsb")
    bk, bk_b = ctx.get("a0_bksb")
    ones_row, _ = ctx.get("ones_row")
    ones_colb, _ = ctx.get("ones_colb")
    tmp, tmp_b = ctx.get("lntmp")
    xlng, xlng_b = ctx.get("xg")
    ALP = nc.allow_low_precision

    emit_a0_tables(ctx)

    P.op("sync", lambda e: e.dma_start(out=bq.ap(), in_=ctx.dram["a0_bqd"].ap()),
         writes=[bq_b], dma=True)
    P.op("sync", lambda e: e.dma_start(out=bk.ap(), in_=ctx.dram["a0_bkd"].ap()),
         writes=[bk_b], dma=True)
    emit_load_w(ctx, "w_a0_q", "wslot0")
    emit_load_w(ctx, "w_a0_k", "wslot1")
    emit_load_w(ctx, "w_a0_v", "wslot2")

    bvr, bvr_b = ctx.get("a0_bvrsb")
    P.op("sync", lambda e: e.dma_start(out=bvr.ap()[0:1, :], in_=ctx.dram["a0_bvrd"].ap()),
         writes=[bvr_b], dma=True)
    psB, psB_b = ctx.ps[7]
    for half in range(2):
        def f_bvb(eng, half=half):
            return eng.matmul(psB.ap(), ones_row.ap(),
                              bvr.ap()[0:1, half * 512:(half + 1) * 512],
                              start=True, stop=True)
        P.op("tensor", f_bvb, reads=[bvr_b], writes=[psB_b])

        def f_bvc(eng, half=half):
            with ALP(reason="bf16 bv"):
                return eng.tensor_copy(bv_bc.ap()[:, half * 512:(half + 1) * 512],
                                       psB.ap())
        P.op("vector", f_bvc, reads=[psB_b], writes=[bv_bc_b])

    P.op("vector", lambda e: e.memset(S.ap(), 0.0), writes=[S_b])
    P.op("vector", lambda e: e.memset(z.ap(), 0.0), writes=[z_b])

    wq, wq_b0 = ctx.get("wslot0")
    wk, wk_b0 = ctx.get("wslot1")
    wv, wv_b0 = ctx.get("wslot2")

    for tg in range(NTG):
        rb = f"rbuf{tg % 2}"
        emit_load_res(ctx, src_dram, tg, rb)
        rbuf, rbuf_b = ctx.get(rb)
        emit_ln_tg(ctx, rb, "xg")

        for wi, (wob, wbb, bb, bbuf, off) in enumerate(
                ((wq, ctx.get("wslot0")[1], bq, bq_b, 0),
                 (wk, ctx.get("wslot1")[1], bk, bk_b, 4096))):
            for do in range(NC):
                psp, psp_b = ctx.ps[do % 2]

                def f_mm(eng, wob=wob, do=do, psp=psp):
                    last = None
                    for dc in range(NC):
                        lhsT = wob.ap()[:, dc * D + do * 128: dc * D + do * 128 + 128]
                        rhs = xlng.ap().rearrange("p (c t) -> p c t", c=NC)[:, dc, :]
                        last = eng.matmul(psp.ap(), lhsT, rhs,
                                          start=(dc == 0), stop=(dc == NC - 1))
                    return last
                P.op("tensor", f_mm, reads=[xlng_b, wbb], writes=[psp_b])

                def f_ev(eng, bb=bb, do=do, off=off, psp=psp):
                    o = grp.ap()[:, off + do * 512: off + do * 512 + 512]
                    return eng.activation(out=o, in_=psp.ap(), func=AF.Relu,
                                          bias=bb.ap()[:, do:do + 1])
                P.op("scalar", f_ev, reads=[psp_b, bbuf], writes=[grp_b])

        for ci in range(4):
            for eh in range(2):
                psv, psv_b = ctx.ps[2 + (ci * 2 + eh) % 2]

                def f_vmm(eng, ci=ci, eh=eh, psv=psv):
                    last = None
                    for dc in range(NC):
                        lhsT = xlng.ap().rearrange("p (c t) -> p c t", c=NC)[:, dc, ci * 128:ci * 128 + 128]
                        rhs = wv.ap()[:, dc * D + eh * 512: dc * D + eh * 512 + 512]
                        last = eng.matmul(psv.ap(), lhsT, rhs,
                                          start=(dc == 0), stop=(dc == NC - 1))
                    return last
                P.op("tensor", f_vmm, reads=[xlng_b, ctx.get("wslot2")[1]], writes=[psv_b])

                def f_vev(eng, ci=ci, eh=eh, psv=psv):
                    o = grp.ap()[:, 8192 + ci * 1024 + eh * 512: 8192 + ci * 1024 + eh * 512 + 512]
                    with ALP(reason="bf16 v evac"):
                        return eng.tensor_tensor(o, psv.ap(),
                                                 bv_bc.ap()[:, eh * 512:(eh + 1) * 512],
                                                 ALU.add)
                P.op("vector", f_vev, reads=[psv_b, bv_bc_b], writes=[grp_b])

        for ci in range(4):
            def qt_ap(dc, ci=ci):
                return grp.ap()[:, dc * 512 + ci * 128: dc * 512 + ci * 128 + 128]

            def kt_ap(dc, ci=ci):
                return grp.ap()[:, 4096 + dc * 512 + ci * 128: 4096 + dc * 512 + ci * 128 + 128]

            def vtm_ap(et, ci=ci):
                return grp.ap()[:, 8192 + ci * 1024 + et * 128: 8192 + ci * 1024 + et * 128 + 128]

            psat, psat_b = ctx.ps[0]

            def f_at(eng, kt_ap=kt_ap, qt_ap=qt_ap):
                last = None
                for dc in range(NC):
                    last = eng.matmul(psat.ap()[:, 0:128], kt_ap(dc), qt_ap(dc),
                                      start=(dc == 0), stop=(dc == NC - 1))
                return last
            P.op("tensor", f_at, reads=[grp_b], writes=[psat_b])

            ab, ab_b = ctx.get("lnx2")

            def f_ab(eng):
                with ALP(reason="bf16 scores"):
                    return eng.tensor_tensor(ab.ap()[:, 0:128], psat.ap()[:, 0:128],
                                             mdec.ap(), ALU.mult)
            P.op("vector", f_ab, reads=[psat_b, mdec_b], writes=[ab_b])

            qd, qd_b = ctx.get("lnxb")

            def f_qt(eng, qt_ap=qt_ap):
                last = None
                for dc in range(NC):
                    last = eng.tensor_tensor(qd.ap()[:, dc * 128:dc * 128 + 128],
                                             qt_ap(dc), qdecb.ap(), ALU.mult)
                return last
            P.op("vector", f_qt, reads=[grp_b, qdecb_b], writes=[qd_b])

            def f_zbf(eng):
                with ALP(reason="bf16 z"):
                    return eng.tensor_copy(zbf.ap(), z.ap())
            P.op("vector", f_zbf, reads=[z_b], writes=[zbf_b])

            psd, psd_b = ctx.ps[3]

            def f_den(eng):
                eng.matmul(psd.ap()[0:1, 0:128], ones_colb.ap(), ab.ap()[:, 0:128],
                           start=True, stop=False)
                last = None
                for dc in range(NC):
                    last = eng.matmul(psd.ap()[0:1, 0:128], zbf.ap()[:, dc:dc + 1],
                                      qd.ap()[:, dc * 128:dc * 128 + 128],
                                      start=False, stop=(dc == NC - 1))
                return last
            P.op("tensor", f_den, reads=[ab_b, qd_b, zbf_b], writes=[psd_b])

            def f_rec(eng):
                eng.tensor_scalar_add(tmp.ap()[0:1, 0:128], psd.ap()[0:1, 0:128], A0EPS)
                return eng.reciprocal(tmp.ap()[0:1, 128:256], tmp.ap()[0:1, 0:128])
            P.op("vector", f_rec, reads=[psd_b], writes=[tmp_b])

            psr, psr_b = ctx.ps[4]

            def f_rbc(eng):
                return eng.matmul(psr.ap()[:, 0:128], ones_row.ap(),
                                  tmp.ap()[0:1, 128:256], start=True, stop=True)
            P.op("tensor", f_rbc, reads=[tmp_b], writes=[psr_b])

            def f_abr(eng):
                with ALP(reason="bf16 scores"):
                    return eng.tensor_tensor(ab.ap()[:, 0:128], ab.ap()[:, 0:128],
                                             psr.ap()[:, 0:128], ALU.mult)
            P.op("vector", f_abr, reads=[ab_b, psr_b], writes=[ab_b])

            def f_qdr(eng):
                last = None
                for dc in range(NC):
                    with ALP(reason="bf16 q"):
                        last = eng.tensor_tensor(qd.ap()[:, dc * 128:dc * 128 + 128],
                                                 qd.ap()[:, dc * 128:dc * 128 + 128],
                                                 psr.ap()[:, 0:128], ALU.mult)
                return last
            P.op("vector", f_qdr, reads=[qd_b, psr_b], writes=[qd_b])

            for ebank in range(2):
                psn, psn_b = ctx.ps[(1, 2)[ebank]]

                def f_num(eng, ebank=ebank, psn=psn, vtm_ap=vtm_ap):
                    last = None
                    for et4 in range(4):
                        et = ebank * 4 + et4
                        o = psn.ap()[:, et4 * 128:et4 * 128 + 128]
                        eng.matmul(o, vtm_ap(et), ab.ap()[:, 0:128], start=True, stop=False)
                        for dc in range(NC):
                            lhsT = S.ap().rearrange("p (c t) -> p c t", c=NC)[:, dc, et * 128:et * 128 + 128]
                            last = eng.matmul(o, lhsT, qd.ap()[:, dc * 128:dc * 128 + 128],
                                              start=False, stop=(dc == NC - 1))
                    return last
                P.op("tensor", f_num, reads=[grp_b, ab_b, qd_b, S_b], writes=[psn_b])

                def f_oadd(eng, ebank=ebank, psn=psn, ci=ci, rbuf=rbuf):
                    last = None
                    for et4 in range(4):
                        et = ebank * 4 + et4
                        o = rbuf.ap().rearrange("p (c t) -> p c t", c=NC)[:, et, ci * 128:ci * 128 + 128]
                        last = eng.tensor_add(o, o, psn.ap()[:, et4 * 128:et4 * 128 + 128])
                    return last
                P.op("vector", f_oadd, reads=[psn_b, rbuf_b], writes=[rbuf_b])

            pst, pst_b = ctx.ps[5]

            def f_ktr(eng, kt_ap=kt_ap):
                last = None
                for dc in range(NC):
                    last = eng.transpose(pst.ap().bitcast(BF16)[:, dc * 128:dc * 128 + 128],
                                         kt_ap(dc), ident.ap())
                return last
            P.op("tensor", f_ktr, reads=[grp_b, ident_b], writes=[pst_b])

            def f_ktv(eng):
                last = None
                for dc in range(NC):
                    last = eng.tensor_scalar_mul(ktm.ap()[:, dc * 128:dc * 128 + 128],
                                                 pst.ap().bitcast(BF16)[:, dc * 128:dc * 128 + 128],
                                                 kdec.ap()[:, 0:1])
                return last
            P.op("vector", f_ktv, reads=[pst_b, kdec_b], writes=[ktm_b])

            psz, psz_b = ctx.ps[6]

            def f_zmm(eng):
                last = None
                for dt in range(NC):
                    last = eng.matmul(psz.ap()[:, dt:dt + 1], ktm.ap()[:, dt * 128:dt * 128 + 128],
                                      ones_colb.ap(), start=True, stop=True)
                return last
            P.op("tensor", f_zmm, reads=[ktm_b], writes=[psz_b])

            def f_zup(eng):
                return eng.scalar_tensor_tensor(z.ap(), z.ap(),
                                                kdec.ap()[:, 1:2], psz.ap()[:, 0:NC],
                                                ALU.mult, ALU.add)
            P.op("vector", f_zup, reads=[z_b, kdec_b, psz_b], writes=[z_b])

            for dt in range(NC):
                for eh in range(2):
                    pss, pss_b = ctx.ps[(5, 7)[eh]]

                    def f_dsmm(eng, dt=dt, eh=eh, pss=pss, ci=ci):
                        return eng.matmul(pss.ap(), ktm.ap()[:, dt * 128:dt * 128 + 128],
                                          grp.ap()[:, 8192 + ci * 1024 + eh * 512: 8192 + ci * 1024 + eh * 512 + 512],
                                          start=True, stop=True)
                    P.op("tensor", f_dsmm, reads=[ktm_b, grp_b], writes=[pss_b])

                    def f_sup(eng, dt=dt, eh=eh, pss=pss):
                        o = S.ap().rearrange("p (c t) -> p c t", c=NC)[:, dt, eh * 512:(eh + 1) * 512]
                        with ALP(reason="bf16 state"):
                            return eng.scalar_tensor_tensor(o, o, kdec.ap()[:, 1:2],
                                                            pss.ap(), ALU.mult, ALU.add)
                    P.op("vector", f_sup, reads=[S_b, kdec_b, pss_b], writes=[S_b])

        emit_store_res(ctx, dst_dram, tg, rb)



A1_MASK_MS = (-384, -256, -128, 0)
B0_MASK_MS = (-384, -256, -128, 0, 128, 256)


def emit_mha_headln(ctx, buf_name, gcol, bcol, n_tg=NTG):
    nc, P = ctx.nc, ctx.P
    buf, buf_b = ctx.get(buf_name)
    e64w, e64w_b = ctx.get("e64w")
    E2w, E2w_b = ctx.get("E2w")
    gqb, gqb_b = ctx.get("gqb")
    ceps, ceps_b = ctx.get("ceps")
    hla, hla_b = ctx.get("hl_a")
    hlb, hlb_b = ctx.get("hl_b")
    hsq, hsq_b = ctx.get("hl_sq")
    ALP = nc.allow_low_precision
    for tg in range(n_tg):
        psA, psA_b = ctx.ps[2]
        psB, psB_b = ctx.ps[3]

        def bsl(tg=tg):
            return buf.ap()[:, tg * 512:(tg + 1) * 512]

        P.op("scalar", lambda e, tg=tg: e.activation(out=hsq.ap(), in_=bsl(tg),
                                                     func=AF.Square),
             reads=[buf_b], writes=[hsq_b])

        def f_sums(eng, tg=tg, psA=psA, psB=psB):
            eng.matmul(psA.ap()[0:64, :], e64w.ap(), bsl(tg), start=True, stop=True)
            return eng.matmul(psB.ap()[0:64, :], e64w.ap(), hsq.ap(), start=True, stop=True)
        P.op("tensor", f_sums, reads=[buf_b, hsq_b, e64w_b], writes=[psA_b, psB_b])

        def f_means(eng, psA=psA, psB=psB):
            eng.tensor_scalar_mul(hla.ap()[0:64, :], psA.ap()[0:64, :], 1.0 / 64)
            return eng.tensor_scalar_mul(hlb.ap()[0:64, :], psB.ap()[0:64, :], 1.0 / 64)
        P.op("vector", f_means, reads=[psA_b, psB_b], writes=[hla_b, hlb_b])

        def f_var(eng, psA=psA):
            eng.tensor_tensor(psA.ap()[0:64, :], hla.ap()[0:64, :], hla.ap()[0:64, :],
                              ALU.mult)
            return eng.tensor_sub(hlb.ap()[0:64, :], hlb.ap()[0:64, :], psA.ap()[0:64, :])
        P.op("vector", f_var, reads=[hla_b, hlb_b], writes=[hlb_b, psA_b])

        def f_rstd(eng):
            eng.activation(out=hlb.ap()[0:64, :], in_=hlb.ap()[0:64, :], func=AF.Ln,
                           bias=ceps.ap()[0:64, 0:1])
            return eng.activation(out=hlb.ap()[0:64, :], in_=hlb.ap()[0:64, :],
                                  func=AF.Exp, scale=-0.5)
        P.op("scalar", f_rstd, reads=[hlb_b, ceps_b], writes=[hlb_b])

        P.op("vector", lambda e: e.scalar_tensor_tensor(
            hla.ap()[0:64, :], hla.ap()[0:64, :], -1.0, hlb.ap()[0:64, :],
            ALU.mult, ALU.mult), reads=[hla_b, hlb_b], writes=[hla_b])

        def f_bc(eng, psA=psA, psB=psB):
            eng.matmul(psA.ap(), E2w.ap()[0:64, :], hlb.ap()[0:64, :],
                       start=True, stop=True)
            return eng.matmul(psB.ap(), E2w.ap()[0:64, :], hla.ap()[0:64, :],
                              start=True, stop=True)
        P.op("tensor", f_bc, reads=[hla_b, hlb_b, E2w_b], writes=[psA_b, psB_b])

        def f_ap1(eng, tg=tg, psA=psA):
            with ALP(reason="bf16 headln"):
                return eng.tensor_tensor(hsq.ap(), bsl(tg), psA.ap(), ALU.mult)
        P.op("vector", f_ap1, reads=[buf_b, psA_b], writes=[hsq_b])

        def f_ap2(eng, psB=psB):
            with ALP(reason="bf16 headln"):
                return eng.tensor_tensor(hsq.ap(), hsq.ap(), psB.ap(), ALU.add)
        P.op("vector", f_ap2, reads=[hsq_b, psB_b], writes=[hsq_b])

        def f_ap3(eng, tg=tg):
            with ALP(reason="bf16 headln"):
                return eng.tensor_scalar(bsl(tg), hsq.ap(),
                                         gqb.ap()[:, gcol:gcol + 1],
                                         gqb.ap()[:, bcol:bcol + 1],
                                         ALU.mult, ALU.add)
        P.op("vector", f_ap3, reads=[hsq_b, gqb_b], writes=[buf_b])


def emit_mha(ctx, src_dram, dst_dram, kind):
    nc, P = ctx.nc, ctx.P
    is_b0 = kind == "b0"
    expb = EXPB_B0 if is_b0 else EXPB_A1
    q_scale = 0.125 if is_b0 else 1.0
    ident, ident_b = ctx.get("ident")
    e64sel, e64sel_b = ctx.get("e64sel")
    tmp, tmp_b = ctx.get("lntmp")
    ALP = nc.allow_low_precision

    msk, msk_b = ctx.get("mskt")
    P.op("sync", lambda e: e.dma_start(out=msk.ap(), in_=ctx.dram[f"msks_{kind}"].ap()),
         writes=[msk_b], dma=True)
    for tg in range(NTG):
        rb = f"rbuf{tg % 2}"
        emit_load_res(ctx, src_dram, tg, rb)
        emit_ln_tg(ctx, rb, f"xln{tg}")

    def pair_block(hp, slot):
        vt, vt_b = ctx.get("vt")
        qt, qt_b = ctx.get("qt")
        kt, kt_b = ctx.get("kt")
        ot, ot_b = ctx.get("otile")
        pball = [ctx.get(f"pb{i}") for i in range(8)]
        wp, wp_b = ctx.get("wpair")
        for wi, dname in enumerate(("w_q", "w_k", "w_v")):
            def f_ldw(e, wi=wi, dname=dname, hp=hp):
                src = ctx.dram[f"{kind}_{dname}"].ap()[:, :, hp * 128:(hp + 1) * 128]
                return e.dma_start(
                    out=wp.ap()[:, wi * NC * 128:(wi + 1) * NC * 128].rearrange("p (c d) -> p c d", c=NC),
                    in_=src.transpose([1, 0, 2]))
            P.op("sync", f_ldw, writes=[wp_b], dma=True)

        def wsl(wi, dc):
            return wp.ap()[:, wi * NC * 128 + dc * 128: wi * NC * 128 + dc * 128 + 128]

        for wi, (dstt, dstb) in enumerate(((qt, ctx.get("qt")[1]), (kt, ctx.get("kt")[1]))):
            for tg in range(NTG):
                psp, psp_b = ctx.ps[tg % 2]
                xl, xl_b = ctx.get(f"xln{tg}")

                def f_mm(eng, wi=wi, tg=tg, psp=psp, xl=xl):
                    last = None
                    for dc in range(NC):
                        last = eng.matmul(psp.ap(), wsl(wi, dc),
                                          xl.ap().rearrange("p (c t) -> p c t", c=NC)[:, dc, :],
                                          start=(dc == 0), stop=(dc == NC - 1))
                    return last
                P.op("tensor", f_mm, reads=[xl_b, wp_b], writes=[psp_b])

                def f_ev(eng, tg=tg, psp=psp, dstt=dstt):
                    return eng.activation(out=dstt.ap()[:, tg * 512:(tg + 1) * 512],
                                          in_=psp.ap(), func=AF.Copy)
                P.op("scalar", f_ev, reads=[psp_b], writes=[dstb])

        if not is_b0:
            emit_mha_headln(ctx, "qt", 0, 1)
            emit_mha_headln(ctx, "kt", 2, 3)

        P.op("vector", lambda e: e.memset(vt.ap(), 1.0), writes=[vt_b])
        vf, vf_b = pball[0]
        for tg in range(NTG):
            psp, psp_b = ctx.ps[2]
            xl, xl_b = ctx.get(f"xln{tg}")

            def f_vmm(eng, tg=tg, psp=psp, xl=xl):
                last = None
                for dc in range(NC):
                    last = eng.matmul(psp.ap(), wsl(2, dc),
                                      xl.ap().rearrange("p (c t) -> p c t", c=NC)[:, dc, :],
                                      start=(dc == 0), stop=(dc == NC - 1))
                return last
            P.op("tensor", f_vmm, reads=[xl_b, wp_b], writes=[psp_b])

            def f_vev(eng, psp=psp):
                return eng.tensor_copy(vf.ap(), psp.ap())
            P.op("vector", f_vev, reads=[psp_b], writes=[vf_b])

            pst, pst_b = ctx.ps[3]
            for c4 in range(4):
                def f_vtr(eng, c4=c4, pst=pst):
                    return eng.transpose(pst.ap().bitcast(BF16)[:, 0:128],
                                         vf.ap()[:, c4 * 128:c4 * 128 + 128], ident.ap())
                P.op("tensor", f_vtr, reads=[vf_b, ident_b], writes=[pst_b])

                def f_vte(eng, tg=tg, c4=c4, pst=pst):
                    ch = tg * 4 + c4
                    eng.tensor_copy(vt.ap()[:, ch * 132:ch * 132 + 64],
                                    pst.ap().bitcast(BF16)[:, 0:64])
                    return eng.tensor_copy(vt.ap()[:, ch * 132 + 66:ch * 132 + 130],
                                           pst.ap().bitcast(BF16)[:, 64:128])
                P.op("vector", f_vte, reads=[pst_b], writes=[vt_b])

        def sc_op(qb, kc):
            scb = [ctx.ps[2 * (kc % 3)], ctx.ps[2 * (kc % 3) + 1]]
            pbs = [pball[2 * (kc % 4)], pball[2 * (kc % 4) + 1]]

            def f_sc(eng, qb=qb, kc=kc):
                last = None
                for h in range(2):
                    last = eng.matmul(
                        scb[h][0].ap(),
                        kt.ap()[h * 64:h * 64 + 64, kc * 128:kc * 128 + 128],
                        qt.ap()[h * 64:h * 64 + 64, qb * 512:(qb + 1) * 512],
                        start=True, stop=True)
                return last
            P.op("tensor", f_sc, reads=[qt_b, kt_b], writes=[scb[0][1], scb[1][1]])

            def f_exp(eng):
                nb, _ = ctx.get("negB")
                bcol = 0 if is_b0 else 1
                last = None
                for h in range(2):
                    last = eng.activation(out=pbs[h][0].ap(), in_=scb[h][0].ap(),
                                          func=AF.Exp, scale=q_scale,
                                          bias=nb.ap()[:, bcol:bcol + 1])
                return last
            P.op("scalar", f_exp, reads=[scb[0][1], scb[1][1], ctx.get("negB")[1]],
                 writes=[pbs[0][1], pbs[1][1]])

            d0 = 512 * qb - 128 * kc
            vs = None
            if is_b0:
                vs = B0_MASK_MS.index(min(d0, 256)) + (0 if hp < 4 else 6)
            elif d0 <= 0:
                vs = A1_MASK_MS.index(d0)
            if vs is not None:
                def f_msk(eng, vs=vs):
                    last = None
                    with ALP(reason="bf16 mask"):
                        for h in range(2):
                            last = eng.tensor_tensor(
                                pbs[h][0].ap(), pbs[h][0].ap(),
                                msk.ap()[:, vs * 512:(vs + 1) * 512], ALU.mult)
                    return last
                P.op("gpsimd", f_msk, reads=[pbs[0][1], pbs[1][1], msk_b],
                     writes=[pbs[0][1], pbs[1][1]])

        def av_op(qb, kc, n_kc):
            av = [ctx.ps[6], ctx.ps[7]]
            pbs = [pball[2 * (kc % 4)], pball[2 * (kc % 4) + 1]]

            def f_av(eng, kc=kc, n_kc=n_kc, av=tuple(x[0] for x in av)):
                last = None
                for h in range(2):
                    last = eng.matmul(
                        av[h].ap()[0:65, :],
                        vt.ap()[:, kc * 132 + h * 66: kc * 132 + h * 66 + 65],
                        pbs[h][0].ap(),
                        start=(kc == 0), stop=(kc == n_kc - 1))
                return last
            P.op("tensor", f_av, reads=[pbs[0][1], pbs[1][1], vt_b],
                 writes=[av[0][1], av[1][1]])

        for qb in range(NTG):
            n_kc = (qb + 1) * 4
            av = [ctx.ps[6], ctx.ps[7]]
            for kc in range(n_kc):
                sc_op(qb, kc)
                if kc >= 2:
                    av_op(qb, kc - 2, n_kc)
            for kc in (n_kc - 2, n_kc - 1):
                if kc >= 0:
                    av_op(qb, kc, n_kc)

            av2, av2_b = ctx.get("av2")

            def f_evac(eng, av=tuple(x[0] for x in av)):
                eng.tensor_copy(av2.ap()[:, 0:512], av[0].ap())
                return eng.tensor_copy(av2.ap()[:, 512:1024], av[1].ap())
            P.op("vector", f_evac, reads=[av[0][1], av[1][1]], writes=[av2_b])

            psb6, psb6_b = ctx.ps[0]
            psb7, psb7_b = ctx.ps[1]

            def f_bc2(eng):
                eng.matmul(psb6.ap()[0:64, :], e64sel.ap()[64:65, :],
                           av2.ap()[64:65, 0:512], start=True, stop=True)
                return eng.matmul(psb7.ap()[0:64, :], e64sel.ap()[64:65, :],
                                  av2.ap()[64:65, 512:1024], start=True, stop=True)
            P.op("tensor", f_bc2, reads=[av2_b, e64sel_b], writes=[psb6_b, psb7_b])

            hla2, hla2_b = ctx.get("hl_a")
            hlb2, hlb2_b = ctx.get("hl_b")

            def f_rcp2(eng):
                eng.reciprocal(hla2.ap()[0:64, :], psb6.ap()[0:64, :])
                return eng.reciprocal(hlb2.ap()[0:64, :], psb7.ap()[0:64, :])
            P.op("vector", f_rcp2, reads=[psb6_b, psb7_b], writes=[hla2_b, hlb2_b])

            for h in range(2):
                rt2, rt2_b = (hla2, hla2_b) if h == 0 else (hlb2, hlb2_b)

                def f_div(eng, h=h, qb=qb, slot=slot, rt2=rt2):
                    o = ot.ap()[h * 64:h * 64 + 64,
                                slot * T + qb * 512: slot * T + qb * 512 + 512]
                    with ALP(reason="bf16 attn out"):
                        return eng.tensor_tensor(o, av2.ap()[0:64, h * 512:h * 512 + 512],
                                                 rt2.ap()[0:64, :], ALU.mult)
                P.op("vector", f_div, reads=[av2_b, rt2_b], writes=[ot_b])

    def out_proj(hp0, first_half):
        wo, wo_b = ctx.get("wosl")
        ot, ot_b = ctx.get("otile")

        def f_ldwo(e, hp0=hp0):
            src = ctx.dram[f"{kind}_w_o"].ap()[hp0:hp0 + 4]
            return e.dma_start(out=wo.ap().rearrange("p (c d) -> p c d", c=4),
                               in_=src.transpose([1, 0, 2]))
        P.op("sync", f_ldwo, writes=[wo_b], dma=True)
        orb, orb_b = ctx.get("orbuf")
        bo, bo_b = ctx.get("bo_sb")
        ldsrc = src_dram if first_half else dst_dram
        for tg in range(NTG):
            def f_ld(e, tg=tg):
                return e.dma_start(
                    out=orb.ap().rearrange("p (c t) -> p c t", c=NC),
                    in_=ldsrc.ap().rearrange("(c p) t -> p c t", p=128)[:, :, tg * 512:(tg + 1) * 512])
            P.op("sync", f_ld, writes=[orb_b], dma=True)
            for do in range(NC):
                psp, psp_b = ctx.ps[do % 2]

                def f_omm(eng, do=do, tg=tg, psp=psp):
                    last = None
                    for hq in range(4):
                        lhsT = wo.ap()[:, hq * D + do * 128: hq * D + do * 128 + 128]
                        rhs = ot.ap()[:, hq * T + tg * 512: hq * T + tg * 512 + 512]
                        last = eng.matmul(psp.ap(), lhsT, rhs,
                                          start=(hq == 0), stop=(hq == 3))
                    return last
                P.op("tensor", f_omm, reads=[ot_b, wo_b], writes=[psp_b])

                if (not is_b0) and (not first_half):
                    def f_add(eng, do=do, psp=psp):
                        o = orb.ap().rearrange("p (c t) -> p c t", c=NC)[:, do, :]
                        return eng.scalar_tensor_tensor(o, psp.ap(),
                                                        bo.ap()[:, do:do + 1], o,
                                                        ALU.add, ALU.add)
                    P.op("vector", f_add, reads=[psp_b, bo_b, orb_b], writes=[orb_b])
                else:
                    def f_add(eng, do=do, psp=psp):
                        o = orb.ap().rearrange("p (c t) -> p c t", c=NC)[:, do, :]
                        return eng.tensor_add(o, o, psp.ap())
                    P.op("vector", f_add, reads=[psp_b, orb_b], writes=[orb_b])

            def f_st(e, tg=tg):
                return e.dma_start(
                    out=dst_dram.ap().rearrange("(c p) t -> p c t", p=128)[:, :, tg * 512:(tg + 1) * 512],
                    in_=orb.ap().rearrange("p (c t) -> p c t", c=NC))
            P.op("gpsimd", f_st, reads=[orb_b], dma=True)

    for half in range(2):
        for sl in range(4):
            pair_block(half * 4 + sl, sl)
        pbn = [f"pb{i}" for i in range(8)]
        toks = P.barrier()
        ctx.renew(toks, "qt", "kt", "vt", "orbuf", "wpair", "wosl", *pbn)
        out_proj(half * 4, half == 0)
        toks = P.barrier()
        ctx.renew(toks, "qt", "kt", "vt", "orbuf", "wpair", "wosl", "otile",
                  *pbn)



def emit_b1(ctx, src_dram, dst_dram):
    nc, P = ctx.nc, ctx.P
    heads, heads_b = ctx.get("hfull")
    xln_s, xln_s_b = ctx.get("b1_xg")
    sm, _ = ctx.get("b1_sm")
    ck_b, sk_b, cq_b, sq_b, kc_b, ks_b, denr_b, th_b = (Buf(n) for n in
        ("ck", "sk", "cq", "sq", "kc", "ks", "denr", "th"))

    def sm_ap(slot, w=512):
        return sm.ap()[0:16, slot * 512:slot * 512 + w]
    th2, th2_b = ctx.get("b1_th2")
    kc2, _ = ctx.get("b1_kc2")
    vt_s, vt_b = ctx.get("b1_vt")
    scin, scin_b = ctx.get("b1_scin")
    cum, cum_b = ctx.get("b1_cum")
    carry, carry_b = ctx.get("b1_carry")
    zer, zer_b = ctx.get("b1_zer")
    hh, hh_b = ctx.get("b1_hh")
    wqk, wqk_b = ctx.get("b1_wqk")
    wv, wv_b = ctx.get("wslot0")
    wo, wo_b = ctx.get("wslot1")
    sel, sel_b = ctx.get("b1_sel")
    cpio2, cpio2_b = ctx.get("b1_cpio2")
    tmp, tmp_b = ctx.get("lntmp")
    ALP = nc.allow_low_precision

    def f_ldwqk(e):
        return e.dma_start(out=wqk.ap().rearrange("p (c d) -> p c d", c=NC),
                           in_=ctx.dram["b1_wqkd"].ap().transpose([1, 0, 2]))
    P.op("sync", f_ldwqk, writes=[wqk_b], dma=True)
    emit_load_w(ctx, "b1_wvd", "wslot0")
    emit_load_w(ctx, "b1_wod", "wslot1")
    P.op("sync", lambda e: e.dma_start(out=sel.ap()[0:16, :],
                                       in_=ctx.dram["b1_seld"].ap()),
         writes=[sel_b], dma=True)
    P.op("vector", lambda e: e.memset(cpio2.ap(), PI / 2), writes=[cpio2_b])
    P.op("vector", lambda e: e.memset(zer.ap(), 0.0), writes=[zer_b])

    for tg in range(NTG):
        rb = f"rbuf{tg % 2}"
        emit_load_res(ctx, src_dram, tg, rb)
        emit_ln_tg(ctx, rb, "b1_xg")

        def xls(dc):
            return xln_s.ap().rearrange("p (c t) -> p c t", c=NC)[:, dc, :]

        psq, psq_b = ctx.ps[0]
        psk, psk_b = ctx.ps[1]

        def f_qk(eng):
            last = None
            for dc in range(NC):
                last = eng.matmul(psq.ap()[0:16, :], wqk.ap()[:, dc * 32:dc * 32 + 16],
                                  xls(dc), start=(dc == 0), stop=(dc == NC - 1))
            for dc in range(NC):
                last = eng.matmul(psk.ap()[0:16, :], wqk.ap()[:, dc * 32 + 16:dc * 32 + 32],
                                  xls(dc), start=(dc == 0), stop=(dc == NC - 1))
            return last
        P.op("tensor", f_qk, reads=[xln_s_b, wqk_b], writes=[psq_b, psk_b])

        def f_tanh(eng):
            eng.activation(out=sm_ap(7), in_=psq.ap()[0:16, :], func=AF.Tanh)
            return eng.activation(out=th2.ap()[0:16, :], in_=psk.ap()[0:16, :], func=AF.Tanh)
        P.op("scalar", f_tanh, reads=[psq_b, psk_b], writes=[th_b, th2_b])

        def f_sins(eng):
            eng.activation(out=sm_ap(0), in_=th2.ap()[0:16, :], func=AF.Sin,
                           scale=PI / 4, bias=cpio2.ap()[0:16, 0:1])
            eng.activation(out=sm_ap(1), in_=th2.ap()[0:16, :], func=AF.Sin, scale=PI / 4)
            eng.activation(out=sm_ap(2), in_=sm_ap(7), func=AF.Sin,
                           scale=PI / 4, bias=cpio2.ap()[0:16, 0:1])
            return eng.activation(out=sm_ap(3), in_=sm_ap(7), func=AF.Sin, scale=PI / 4)
        P.op("scalar", f_sins, reads=[th_b, th2_b, cpio2_b],
             writes=[ck_b, sk_b, cq_b, sq_b])

        def f_kcum(eng, tg=tg):
            ini_c = 0.0 if tg == 0 else carry.ap()[0:16, 16:17]
            ini_s = 0.0 if tg == 0 else carry.ap()[0:16, 17:18]
            eng.tensor_tensor_scan(kc2.ap()[0:16, 0:512], sm_ap(0), zer.ap()[0:16, 0:512],
                                   ini_c, ALU.add, ALU.add)
            eng.tensor_tensor_scan(kc2.ap()[0:16, 512:1024], sm_ap(1), zer.ap()[0:16, 0:512],
                                   ini_s, ALU.add, ALU.add)
            eng.tensor_copy(carry.ap()[0:16, 16:17], kc2.ap()[0:16, 511:512])
            return eng.tensor_copy(carry.ap()[0:16, 17:18], kc2.ap()[0:16, 1023:1024])
        P.op("vector", f_kcum, reads=[ck_b, sk_b, zer_b, carry_b],
             writes=[kc_b, ks_b, carry_b])

        def f_den(eng):
            eng.tensor_tensor(tmp.ap()[0:16, :], sm_ap(2), kc2.ap()[0:16, 0:512], ALU.mult)
            eng.tensor_tensor(hh.ap()[0:16, 0:512], sm_ap(3), kc2.ap()[0:16, 512:1024], ALU.mult)
            eng.tensor_add(tmp.ap()[0:16, :], tmp.ap()[0:16, :], hh.ap()[0:16, 0:512])
            eng.tensor_scalar_add(tmp.ap()[0:16, :], tmp.ap()[0:16, :], 1e-6)
            with ALP(reason="bf16 reciprocal"):
                return eng.reciprocal(sm_ap(6), tmp.ap()[0:16, :])
        P.op("vector", f_den, reads=[cq_b, sq_b, kc_b, ks_b],
             writes=[denr_b, tmp_b, hh_b])

        for cc in range(NC):
            psv, psv_b = ctx.ps[cc % 2]

            def f_vmm(eng, cc=cc, psv=psv):
                last = None
                for dc in range(NC):
                    lhsT = wv.ap()[:, dc * D + cc * 128: dc * D + cc * 128 + 128]
                    last = eng.matmul(psv.ap(), lhsT, xls(dc),
                                      start=(dc == 0), stop=(dc == NC - 1))
                return last
            P.op("tensor", f_vmm, reads=[xln_s_b, ctx.get("wslot0")[1]], writes=[psv_b])

            def f_vev(eng, psv=psv):
                return eng.tensor_copy(vt_s.ap(), psv.ap())
            P.op("vector", f_vev, reads=[psv_b], writes=[vt_b])

            psc, psc_b = ctx.ps[2]
            pss, pss_b = ctx.ps[3]

            def f_bc(eng, cc=cc):
                eng.matmul(psc.ap(), sel.ap()[0:16, cc * 128:cc * 128 + 128],
                           sm_ap(0), start=True, stop=True)
                return eng.matmul(pss.ap(), sel.ap()[0:16, cc * 128:cc * 128 + 128],
                                  sm_ap(1), start=True, stop=True)
            P.op("tensor", f_bc, reads=[sel_b, ck_b, sk_b], writes=[psc_b, pss_b])

            def f_sc(eng):
                eng.tensor_tensor(scin.ap()[:, 0:512], vt_s.ap(), psc.ap(), ALU.mult)
                return eng.tensor_tensor(scin.ap()[:, 512:1024], vt_s.ap(), pss.ap(), ALU.mult)
            P.op("vector", f_sc, reads=[vt_b, psc_b, pss_b], writes=[scin_b])

            def f_scan(eng, cc=cc, tg=tg):
                ini_c = 0.0 if tg == 0 else carry.ap()[:, 2 * cc:2 * cc + 1]
                ini_s = 0.0 if tg == 0 else carry.ap()[:, 2 * cc + 1:2 * cc + 2]
                eng.tensor_tensor_scan(cum.ap()[:, 0:512], scin.ap()[:, 0:512],
                                       zer.ap()[:, 0:512], ini_c, ALU.add, ALU.add)
                eng.tensor_tensor_scan(cum.ap()[:, 512:1024], scin.ap()[:, 512:1024],
                                       zer.ap()[:, 0:512], ini_s, ALU.add, ALU.add)
                eng.tensor_copy(carry.ap()[:, 2 * cc:2 * cc + 1], cum.ap()[:, 511:512])
                return eng.tensor_copy(carry.ap()[:, 2 * cc + 1:2 * cc + 2], cum.ap()[:, 1023:1024])
            P.op("vector", f_scan, reads=[scin_b, zer_b, carry_b], writes=[cum_b, carry_b])

            psq2, psq2_b = ctx.ps[2]
            pss2, pss2_b = ctx.ps[3]
            psr, psr_b = ctx.ps[4]

            def f_bc2(eng, cc=cc):
                eng.matmul(psq2.ap(), sel.ap()[0:16, cc * 128:cc * 128 + 128],
                           sm_ap(2), start=True, stop=True)
                eng.matmul(pss2.ap(), sel.ap()[0:16, cc * 128:cc * 128 + 128],
                           sm_ap(3), start=True, stop=True)
                return eng.matmul(psr.ap(), sel.ap()[0:16, cc * 128:cc * 128 + 128],
                                  sm_ap(6), start=True, stop=True)
            P.op("tensor", f_bc2, reads=[sel_b, cq_b, sq_b, denr_b],
                 writes=[psq2_b, pss2_b, psr_b])

            def f_h(eng, cc=cc, tg=tg):
                eng.tensor_tensor(hh.ap()[:, 0:512], cum.ap()[:, 0:512], psq2.ap(), ALU.mult)
                eng.tensor_tensor(hh.ap()[:, 512:1024], cum.ap()[:, 512:1024], pss2.ap(), ALU.mult)
                eng.tensor_add(hh.ap()[:, 0:512], hh.ap()[:, 0:512], hh.ap()[:, 512:1024])
                o = heads.ap().rearrange("p (c t) -> p c t", c=NC)[:, cc, tg * 512:(tg + 1) * 512]
                with ALP(reason="bf16 heads"):
                    return eng.tensor_tensor(o, hh.ap()[:, 0:512], psr.ap(), ALU.mult)
            P.op("vector", f_h, reads=[cum_b, psq2_b, pss2_b, psr_b],
                 writes=[hh_b, heads_b])

    for tg in range(NTG):
        rb = f"rbuf{tg % 2}"
        emit_load_res(ctx, src_dram, tg, rb)
        rbuf, rbuf_b = ctx.get(rb)
        for do in range(NC):
            psh, psb = ctx.ps[do % 2]

            def f_omm(eng, do=do, tg=tg, psh=psh):
                last = None
                for cc in range(NC):
                    lhsT = wo.ap()[:, cc * D + do * 128: cc * D + do * 128 + 128]
                    rhs = heads.ap().rearrange("p (c t) -> p c t", c=NC)[:, cc, tg * 512:(tg + 1) * 512]
                    last = eng.matmul(psh.ap(), lhsT, rhs, start=(cc == 0), stop=(cc == NC - 1))
                return last
            P.op("tensor", f_omm, reads=[heads_b, ctx.get("wslot1")[1]], writes=[psb])

            def f_add(eng, do=do, psh=psh, rbuf=rbuf):
                o = rbuf.ap().rearrange("p (c t) -> p c t", c=NC)[:, do, :]
                return eng.tensor_add(o, o, psh.ap())
            P.op("vector", f_add, reads=[psb, rbuf_b], writes=[rbuf_b])
        emit_store_res(ctx, dst_dram, tg, rb)



def _wmat(w):
    import ml_dtypes
    din, dout = w.shape
    return np.ascontiguousarray(w.reshape(NC, 128, dout)).astype(ml_dtypes.bfloat16)


def _wvec(v):
    return np.ascontiguousarray(np.asarray(v, np.float32).reshape(NC, 128).T)



STAGE_NAMES = ["a0", "ffn0a", "b0", "ffn0b", "a1", "ffn1a", "b1", "ffn1b"]

DRAM_SPECS = {
    "common": [("identd", [128, 128], BF16)],
    "ffn0": [("w_ffn0_w", [NC, 128, D], BF16), ("w_ffn0_a1", [NC, 128, D], BF16),
             ("w_ffn0_a2", [NC, 128, D], BF16), ("b_ffn0d", [128, NC], F32)],
    "ffn1": [("w_ffn1_r", [NC, 128, D], BF16), ("w_ffn1_k", [NC, 128, D], BF16),
             ("w_ffn1_v", [NC, 128, D], BF16), ("mu_rd", [128, NC], F32),
             ("mu_kd", [128, NC], F32)],
    "a0": [("w_a0_q", [NC, 128, D], BF16), ("w_a0_k", [NC, 128, D], BF16),
           ("w_a0_v", [NC, 128, D], BF16), ("a0_bqd", [128, NC], F32),
           ("a0_bkd", [128, NC], F32), ("a0_bvrd", [1, D], F32),
           ("a0_gam", [1, 2], F32), ("cmaskTd", [128, 128], BF16)],
    "b0": [("b0_w_q", [NC, 128, D], BF16), ("b0_w_k", [NC, 128, D], BF16),
           ("b0_w_v", [NC, 128, D], BF16), ("b0_w_o", [NC, 128, D], BF16),
           ("msks_b0", [128, 16 * 512], BF16), ("e64seld", [128, 64], F32)],
    "a1": [("a1_w_q", [NC, 128, D], BF16), ("a1_w_k", [NC, 128, D], BF16),
           ("a1_w_v", [NC, 128, D], BF16), ("a1_w_o", [NC, 128, D], BF16),
           ("msks_a1", [128, 16 * 512], BF16), ("e64seld", [128, 64], F32),
           ("e64wd", [128, 64], BF16), ("E2wd", [64, 128], F32),
           ("gqbd", [128, 4], F32), ("bod", [128, NC], F32)],
    "b1": [("b1_wqkd", [NC, 128, 32], BF16), ("b1_wvd", [NC, 128, D], BF16),
           ("b1_wod", [NC, 128, D], BF16), ("b1_seld", [16, 8 * 128], BF16)],
}


def build_program(lo, hi):
    import contextlib
    nc = bass.Bass("TRN2", target_bir_lowering=False, debug=False)
    x_in = nc.dram_tensor("x", [D, T], F32, kind="ExternalInput")
    y_out = nc.dram_tensor("y", [D, T], F32, kind="ExternalOutput")
    res = nc.dram_tensor("resid", [D, T], F32, kind="Internal")

    stages = STAGE_NAMES[lo:hi]
    kinds = set(s.rstrip("ab") for s in stages)

    dram = {}
    for k in sorted(kinds | {"common"}):
        for (dn, shape, dt) in DRAM_SPECS.get(k, []):
            if dn not in dram:
                dram[dn] = nc.dram_tensor(dn, shape, dt, kind="ExternalInput")

    with contextlib.ExitStack() as st:
        prog = Prog(nc)
        ctx = Ctx(nc, prog)
        ctx.dram = dram

        slab_w = st.enter_context(nc.sbuf_tensor("slab_w", [128, 8192], F32))
        slab_1 = st.enter_context(nc.sbuf_tensor("slab_1", [128, 16384], F32))
        slab_2 = st.enter_context(nc.sbuf_tensor("slab_2", [128, 16384], F32))
        slab_3 = st.enter_context(nc.sbuf_tensor("slab_3", [128, 6400], F32))
        for i in range(8):
            h = st.enter_context(nc.psum_tensor(f"psb{i}", [128, 512], F32))
            ctx.ps.append((h, Buf(f"psb{i}")))
        lntmp = st.enter_context(nc.sbuf_tensor("lntmp", [128, 512], F32))
        lnv2 = st.enter_context(nc.sbuf_tensor("lnv2", [128, 512], F32))
        onesc = st.enter_context(nc.sbuf_tensor("onesc", [128, 1], BF16))
        onesc64 = st.enter_context(nc.sbuf_tensor("onesc64", [128, 64], BF16))
        onesr = st.enter_context(nc.sbuf_tensor("onesr", [1, 128], F32))
        consts = st.enter_context(nc.sbuf_tensor("consts", [1, 8], F32))
        smalls = st.enter_context(nc.sbuf_tensor("smalls", [128, 256], F32))
        hlc = st.enter_context(nc.sbuf_tensor("hlc", [128, 160], F32))

        ctx.direct("lntmp", lntmp)
        ctx.direct("lnv2", lnv2)
        ctx.direct("ones_colb", onesc)
        ctx.direct("ones_c64", onesc64)
        ctx.direct("ones_row", onesr)
        ctx.direct("consts", consts)

        ctx.tile("rbuf0", slab_2, F32, 0, NC * 512)
        ctx.tile("rbuf1", slab_2, F32, 4096, NC * 512)
        ctx.tile("lnx2", slab_2, BF16, 8192, NC * 512)
        ctx.tile("lnxb", slab_2, BF16, 10240, NC * 512)
        ctx.tile("xg", slab_3, BF16, 0, NC * 512)
        ctx.tile("xp", slab_3, BF16, 2048, NC * 512)
        ctx.tile("mixg", slab_3, BF16, 4096, NC * 512)
        ctx.tile("tg_out", slab_2, BF16, 12288, NC * 512)
        ctx.tile("carry8", slab_3, BF16, 6144, NC)
        ctx.tile("hfull", slab_1, BF16, 0, NC * T)
        ctx.tile("wslot2", slab_2, BF16, 12288, NC * D)
        ctx.tile("a0_grp", slab_1, BF16, 0, 12288)
        ctx.tile("a0_S", slab_1, BF16, 6144, NC * 1024)
        ctx.tile("a0_tab", slab_1, F32, 10240, 1024)
        ctx.tile("a0_bvbc", slab_1, BF16, 11264, 1024)
        ctx.tile("a0_ktm", slab_1, BF16, 11776, 1024)
        ctx.tile("a0_mdec", slab_1, BF16, 12288, 128)
        ctx.tile("a0_qdecb", slab_1, BF16, 12352, 128)
        ctx.tile("a0_kdec", slab_1, F32, 12416, 2)
        ctx.tile("a0_z", slab_1, F32, 12424, NC)
        ctx.tile("a0_zbf", slab_1, BF16, 12432, NC)
        ctx.tile("a0_cmask", slab_1, BF16, 12504, 128)
        ctx.tile("a0_bvrsb", slab_1, F32, 12568, D)
        ctx.tile("a0_gamsb", smalls, F32, 24, 2, scratch=False)
        ctx.tile("a0_bqsb", smalls, F32, 32, NC, scratch=False)
        ctx.tile("a0_bksb", smalls, F32, 40, NC, scratch=False)
        for tg in range(NTG):
            ctx.tile(f"xln{tg}", slab_1, BF16, tg * 2048, NC * 512)
        ctx.tile("qt", slab_3, BF16, 0, 4096)
        ctx.tile("kt", slab_3, BF16, 2048, 4096)
        ctx.tile("vt", slab_3, BF16, 4096, 32 * 132)
        ctx.tile("av2", slab_2, F32, 14592, 1024)
        ctx.tile("orbuf", slab_3, F32, 0, NC * 512)
        ctx.tile("otile", slab_2, BF16, 0, 4 * T)
        ctx.tile("pb0", slab_2, BF16, 12288, 512)
        ctx.tile("pb1", slab_2, BF16, 12544, 512)
        ctx.tile("pb2", slab_2, BF16, 15616, 512)
        ctx.tile("pb3", slab_2, BF16, 15872, 512)
        ctx.tile("pb4", slab_2, BF16, 13824, 512)
        ctx.tile("pb5", slab_2, BF16, 14080, 512)
        ctx.tile("pb6", slab_2, BF16, 14336, 512)
        ctx.tile("pb7", slab_2, BF16, 16128, 512)
        ctx.tile("hl_a", slab_2, F32, 12800, 512)
        ctx.tile("hl_b", slab_2, F32, 13312, 512)
        ctx.tile("hl_sq", slab_2, BF16, 13824, 512)
        ctx.tile("hl_gb", slab_2, BF16, 14080, 1024)
        ctx.tile("wpair", slab_w, BF16, 0, 3 * NC * 128)
        ctx.tile("wosl", slab_w, BF16, 0, 4 * D)
        ctx.tile("mskt", slab_w, BF16, 4096, 16 * 512)
        ctx.tile("b1_sm", slab_3, BF16, 0, 8 * 512)
        ctx.tile("b1_th2", slab_3, BF16, 2048, 512)
        ctx.tile("b1_kc2", slab_3, F32, 2304, 1024)
        ctx.tile("b1_vt", slab_3, BF16, 3328, 512)
        ctx.tile("b1_scin", slab_3, BF16, 3584, 1024)
        ctx.tile("b1_cum", slab_3, F32, 4096, 1024)
        ctx.tile("b1_carry", slab_3, F32, 5120, 20)
        ctx.tile("b1_zer", slab_3, BF16, 5140, 512)
        ctx.tile("b1_wqk", slab_3, BF16, 5396, NC * 32)
        ctx.tile("b1_cpio2", slab_3, F32, 5524, 1)
        ctx.tile("b1_sel", slab_3, BF16, 5526, 8 * 128)
        ctx.tile("b1_hh", slab_2, F32, 12800, 1024)
        ctx.tile("b1_xg", slab_2, BF16, 13824, NC * 512)
        ctx.tile("gqb", smalls, F32, 48, 4, scratch=False)
        ctx.tile("ceps", smalls, F32, 52, 1, scratch=False)
        ctx.tile("bo_sb", smalls, F32, 53, NC, scratch=False)
        ctx.tile("e64", smalls, BF16, 61, 2, scratch=False)
        ctx.tile("e64sel", smalls, F32, 62, 64, scratch=False)
        ctx.tile("E2t", smalls, BF16, 126, 128, scratch=False)
        ctx.tile("ident", smalls, BF16, 190, 128, scratch=False)
        ctx.tile("negB", smalls, F32, 254, 2, scratch=False)
        ctx.tile("e64w", hlc, BF16, 0, 64, scratch=False)
        ctx.tile("E2w", hlc, F32, 32, 128, scratch=False)
        ctx.tile("wslot0", slab_w, BF16, 0, NC * D)
        ctx.tile("wslot1", slab_w, BF16, 4096, NC * D)
        ctx.tile("b_ffn0", smalls, F32, 0, NC, scratch=False)
        ctx.tile("v_mu_r", smalls, F32, 8, NC, scratch=False)
        ctx.tile("v_mu_k", smalls, F32, 16, NC, scratch=False)

        P = prog
        P.op("vector", lambda e: e.memset(onesc.ap(), 1.0),
             writes=[ctx.get("ones_colb")[1]])
        P.op("vector", lambda e: e.memset(onesc64.ap(), 1.0),
             writes=[ctx.get("ones_c64")[1]])
        P.op("vector", lambda e: e.memset(onesr.ap(), 1.0),
             writes=[ctx.get("ones_row")[1]])

        def f_csts(e):
            e.memset(consts.ap()[0:1, 0:1], 1e-5)
            return e.memset(consts.ap()[0:1, 1:2], 1e-6)
        P.op("vector", f_csts, writes=[ctx.get("consts")[1]])
        P.op("vector", lambda e: e.memset(lntmp.ap(), 0.0),
             writes=[ctx.get("lntmp")[1]])
        P.op("vector", lambda e: e.memset(ctx.get("ceps")[0].ap(), 1e-5),
             writes=[ctx.get("ceps")[1]])

        def f_negb(e):
            e.memset(ctx.get("negB")[0].ap()[:, 0:1], -EXPB_B0)
            return e.memset(ctx.get("negB")[0].ap()[:, 1:2], -EXPB_A1)
        P.op("vector", f_negb, writes=[ctx.get("negB")[1]])
        P.op("sync", lambda e: e.dma_start(out=ctx.get("ident")[0].ap(),
                                           in_=dram["identd"].ap()),
             writes=[ctx.get("ident")[1]], dma=True)
        if kinds & {"b0", "a1"}:
            P.op("sync", lambda e: e.dma_start(out=ctx.get("e64sel")[0].ap(),
                                               in_=dram["e64seld"].ap()),
                 writes=[ctx.get("e64sel")[1]], dma=True)
        if "a1" in kinds:
            P.op("sync", lambda e: e.dma_start(out=ctx.get("e64w")[0].ap(),
                                               in_=dram["e64wd"].ap()),
                 writes=[ctx.get("e64w")[1]], dma=True)
            P.op("sync", lambda e: e.dma_start(out=ctx.get("E2w")[0].ap()[0:64, :],
                                               in_=dram["E2wd"].ap()),
                 writes=[ctx.get("E2w")[1]], dma=True)
            P.op("sync", lambda e: e.dma_start(out=ctx.get("gqb")[0].ap(),
                                               in_=dram["gqbd"].ap()),
                 writes=[ctx.get("gqb")[1]], dma=True)
            P.op("sync", lambda e: e.dma_start(out=ctx.get("bo_sb")[0].ap(),
                                               in_=dram["bod"].ap()),
                 writes=[ctx.get("bo_sb")[1]], dma=True)

        if "ffn0" in kinds:
            def f_ldb(e):
                return e.dma_start(out=ctx.get("b_ffn0")[0].ap(),
                                   in_=dram["b_ffn0d"].ap())
            P.op("sync", f_ldb, writes=[ctx.get("b_ffn0")[1]], dma=True)
        if "ffn1" in kinds:
            P.op("sync", lambda e: e.dma_start(out=ctx.get("v_mu_r")[0].ap(),
                                               in_=dram["mu_rd"].ap()),
                 writes=[ctx.get("v_mu_r")[1]], dma=True)
            P.op("sync", lambda e: e.dma_start(out=ctx.get("v_mu_k")[0].ap(),
                                               in_=dram["mu_kd"].ap()),
                 writes=[ctx.get("v_mu_k")[1]], dma=True)

        cur_src = x_in
        for si, sname in enumerate(stages):
            last = si == len(stages) - 1
            dst = y_out if last else res
            kind = sname.rstrip("ab")
            if kind == "ffn0":
                emit_ffn0(ctx, cur_src, dst)
            elif kind == "ffn1":
                emit_ffn1(ctx, cur_src, dst)
            elif kind == "a0":
                emit_a0(ctx, cur_src, dst)
            elif kind in ("b0", "a1"):
                emit_mha(ctx, cur_src, dst, kind)
            elif kind == "b1":
                emit_b1(ctx, cur_src, dst)
            else:
                raise NotImplementedError(sname)
            cur_src = res if not last else None
            if not last:
                toks = P.barrier()
                ctx.renew(toks, *ctx.scratch)

        P.op("gpsimd", lambda e: e.engine_nop(),
             after=[("gD", prog.plan["gD"], "gpsimd", True)])

        build_and_finalize(nc, prog)
    return nc


_PROG_CACHE = {}

DBG_RANGE = (0, 8)
DBG_X0 = None


def _staged_inputs(inputs):
    g = lambda n: np.asarray(inputs[n], np.float32)
    d = {}
    d["w_ffn0_w"] = _wmat(g("L0_ffn_weight"))
    d["w_ffn0_a1"] = _wmat(g("L0_ffn_aux_weight1"))
    d["w_ffn0_a2"] = _wmat(g("L0_ffn_aux_weight2"))
    d["b_ffn0d"] = _wvec(g("L0_ffn_bias"))
    d["w_ffn1_r"] = _wmat(g("L1_ffn_W_r").T)
    d["w_ffn1_k"] = _wmat(g("L1_ffn_W_k").T)
    d["w_ffn1_v"] = _wmat(g("L1_ffn_W_v").T)
    d["mu_rd"] = _wvec(g("L1_ffn_mu_r"))
    d["mu_kd"] = _wvec(g("L1_ffn_mu_k"))
    import ml_dtypes
    r128 = np.arange(128)
    d["w_a0_q"] = _wmat(g("L0_attn_a_Wq").T)
    d["w_a0_k"] = _wmat(g("L0_attn_a_Wk").T)
    d["w_a0_v"] = _wmat(g("L0_attn_a_Wv").T)
    d["a0_bqd"] = _wvec(g("L0_attn_a_bq"))
    d["a0_bkd"] = _wvec(g("L0_attn_a_bk"))
    d["a0_bvrd"] = np.ascontiguousarray(g("L0_attn_a_bv").reshape(1, D))
    gamma = float(np.asarray(inputs["L0_attn_a_gamma"]))
    d["a0_gam"] = np.array([[gamma, 0.0]], np.float32)
    d["identd"] = np.eye(128, dtype=ml_dtypes.bfloat16)
    d["cmaskTd"] = (r128[None, :] >= r128[:, None]).astype(ml_dtypes.bfloat16)
    d["b0_w_q"] = _wmat(g("L0_attn_b_Wq").T)
    d["b0_w_k"] = _wmat(g("L0_attn_b_Wk").T)
    d["b0_w_v"] = _wmat(g("L0_attn_b_Wv").T)
    d["b0_w_o"] = _wmat(g("L0_attn_b_Wo").T)
    d["a1_w_q"] = _wmat(g("L1_attn_a_Wq").T)
    d["a1_w_k"] = _wmat(g("L1_attn_a_Wk").T)
    d["a1_w_v"] = _wmat(g("L1_attn_a_Wv").T)
    d["a1_w_o"] = _wmat(g("L1_attn_a_Wo").T)
    gq, bq1 = g("L1_attn_a_gq"), g("L1_attn_a_bq")
    gk, bk1 = g("L1_attn_a_gk"), g("L1_attn_a_bk")
    d["gqbd"] = np.stack([np.tile(gq, 2), np.tile(bq1, 2),
                          np.tile(gk, 2), np.tile(bk1, 2)], axis=1).astype(np.float32)
    d["bod"] = _wvec(g("L1_attn_a_bo"))
    e64w = np.zeros((128, 64), np.float32)
    e64w[0:64, 0:32] = 1; e64w[64:128, 32:64] = 1
    d["e64wd"] = e64w.astype(ml_dtypes.bfloat16)
    E2w = np.zeros((64, 128), np.float32)
    E2w[0:32, 0:64] = 1.0 / 32; E2w[32:64, 64:128] = 1.0 / 32
    d["E2wd"] = E2w
    e64s = np.zeros((128, 64), np.float32)
    e64s[64, :] = 1.0
    d["e64seld"] = e64s
    r = np.arange(128)[:, None]
    c = np.arange(512)[None, :]
    mb0 = np.zeros((128, 16 * 512), np.float32)
    for i, m in enumerate(B0_MASK_MS):
        delta = m + c - r
        causal = delta >= 0
        mb0[:, i * 512:(i + 1) * 512] = causal & ((delta < 64) | (delta % 64 == 0))
        sameblock = (r // 64) == ((m + c) // 64)
        mb0[:, (6 + i) * 512:(7 + i) * 512] = causal & (sameblock | (r % 64 == 63))
    d["msks_b0"] = mb0.astype(ml_dtypes.bfloat16)
    ma1 = np.zeros((128, 16 * 512), np.float32)
    for i, m in enumerate(A1_MASK_MS):
        ma1[:, i * 512:(i + 1) * 512] = (m + c - r) >= 0
    d["msks_a1"] = ma1.astype(ml_dtypes.bfloat16)
    WQ2 = g("L1_attn_b_W_Q")[:, :, 0].T
    WK2 = g("L1_attn_b_W_K")[:, :, 0].T
    WV2 = np.ascontiguousarray(g("L1_attn_b_W_V").transpose(1, 0, 2).reshape(D, D))
    WO2 = np.ascontiguousarray(g("L1_attn_b_W_O").transpose(2, 1, 0).reshape(D, D))
    d["b1_wqkd"] = _wmat(np.concatenate([WQ2, WK2], axis=1))
    d["b1_wvd"] = _wmat(WV2)
    d["b1_wod"] = _wmat(WO2)
    selmat = np.zeros((16, 8 * 128), np.float32)
    for cc in range(8):
        for p in range(128):
            selmat[2 * cc + (p >= 64), cc * 128 + p] = 1.0
    d["b1_seld"] = selmat.astype(ml_dtypes.bfloat16)
    return d


def kernel(**inputs):
    _ = inputs.get("L1_attn_a_mask")
    lo, hi = DBG_RANGE
    key = (lo, hi)
    if key not in _PROG_CACHE:
        _PROG_CACHE[key] = build_program(lo, hi)
    nc = _PROG_CACHE[key]
    wd = _staged_inputs(inputs)
    x = np.asarray(inputs["L0_attn_a_x_seq"], np.float32)
    if DBG_X0 is not None:
        x = np.asarray(DBG_X0, np.float32)
    stages = STAGE_NAMES[lo:hi]
    kinds = set(s.rstrip("ab") for s in stages)
    need = set()
    for k in kinds | {"common"}:
        for (dn, _s, _d) in DRAM_SPECS.get(k, []):
            need.add(dn)
    in_maps = []
    for c in range(8):
        b = c % 4
        m = {"x": np.ascontiguousarray(x[b].T)}
        for dn in need:
            m[dn] = wd[dn]
        in_maps.append(m)
    res = run_bass_kernel_spmd(nc, in_maps, core_ids=list(range(8)))
    out = np.zeros((B, T, D), np.float32)
    for b in range(4):
        out[b] = res.results[b]["y"].T
    return out


# version 12 (final)
# speedup vs baseline: 1.1997x; 1.1997x over previous
import math
import numpy as np
import concourse.bass as bass
import concourse.mybir as mybir
from concourse.bass_utils import run_bass_kernel_spmd

F32 = mybir.dt.float32
BF16 = mybir.dt.bfloat16
AF = mybir.ActivationFunctionType
ALU = mybir.AluOpType
AX = mybir.AxisListType

B, T, D = 4, 4096, 1024
NC = 8
NTG = T // 512
PI = float(np.pi)
EXPB_B0 = 20.0
EXPB_A1 = 50.0

ENGINES = ("sync", "tensor", "vector", "scalar", "gpsimd")


class Buf:
    __slots__ = ("w", "readers", "name")

    def __init__(self, name=""):
        self.w = None
        self.readers = []
        self.name = name

    def read_deps(self):
        return [self.w] if self.w is not None else []

    def write_deps(self):
        deps = [self.w] if self.w is not None else []
        return deps + self.readers


class Prog:
    def __init__(self, nc):
        self.nc = nc
        self.ops = {e: [] for e in ENGINES}
        self.engine_sem = {"tensor": "tC", "vector": "vC", "scalar": "sC",
                          "gpsimd": "gC"}
        self.dma_sem = {"sync": "syD", "gpsimd": "gD", "scalar": "scD"}
        self.sem_names = ["tC", "vC", "sC", "gC", "syD", "gD", "scD"]
        self.plan = {s: 0 for s in self.sem_names}
        self.n_ops = 0

    def cur_tokens(self):
        toks = []
        for e in ("tensor", "vector", "scalar", "gpsimd"):
            s = self.engine_sem[e]
            if self.plan[s]:
                toks.append((s, self.plan[s], e, False))
        for e, s in self.dma_sem.items():
            if self.plan[s]:
                toks.append((s, self.plan[s], e, True))
        return toks

    def op(self, engine, fn, reads=(), writes=(), after=(), dma=False):
        assert engine in ENGINES
        if dma:
            sem = self.dma_sem[engine]
            inc = 16
        else:
            sem = self.engine_sem[engine]
            inc = 1
        deps = set(after)
        for b in reads:
            deps.update(b.read_deps())
        for b in writes:
            deps.update(b.write_deps())
        waits = {}
        for (s, v, peng, pdma) in deps:
            assert v <= self.plan[s], "dependency on future op"
            if peng == engine and not pdma:
                continue
            waits[s] = max(waits.get(s, 0), v)
        if dma and self.plan[sem]:
            waits[sem] = max(waits.get(sem, 0), self.plan[sem])
        wl = [(s, v) for (s, v) in waits.items()]
        self.plan[sem] += inc
        token = (sem, self.plan[sem], engine, dma)
        for b in reads:
            b.readers.append(token)
        for b in writes:
            b.w = token
            b.readers = []
        self.ops[engine].append((wl, fn, sem, inc))
        self.n_ops += 1
        return token

    def barrier(self):
        return self.cur_tokens()

    def make_emitter(self, eng_name, sems):
        oplist = self.ops[eng_name]

        def emitter(eng):
            hwm = {}
            for (wl, fn, semname, inc) in oplist:
                for (s, v) in wl:
                    if hwm.get(s, 0) >= v:
                        continue
                    eng.wait_ge(sems[s], v)
                    hwm[s] = v
                last = fn(eng)
                assert last is not None, f"op on {eng_name} returned None"
                last.then_inc(sems[semname], inc)
        return emitter


def build_and_finalize(nc, prog):
    import contextlib
    with contextlib.ExitStack() as st:
        sems = {}
        for s in prog.sem_names:
            sems[s] = st.enter_context(nc.semaphore(s))
        blk = st.enter_context(nc.Block())
        blk.sync(prog.make_emitter("sync", sems))
        blk.tensor(prog.make_emitter("tensor", sems))
        blk.vector(prog.make_emitter("vector", sems))
        blk.scalar(prog.make_emitter("scalar", sems))
        blk.gpsimd(prog.make_emitter("gpsimd", sems))


class Tile:

    def __init__(self, slab, dtype, word_off, cols):
        self.slab = slab
        self.dtype = dtype
        self.word_off = word_off
        self.cols = cols

    def ap(self):
        a = self.slab.ap()
        if self.dtype == F32:
            return a[:, self.word_off:self.word_off + self.cols]
        a = a.bitcast(BF16)
        return a[:, 2 * self.word_off:2 * self.word_off + self.cols]


class Ctx:
    def __init__(self, nc, prog):
        self.nc = nc
        self.P = prog
        self.sb = {}
        self.dram = {}
        self.ps = []
        self.scratch = []

    def tile(self, name, slab, dtype, word_off, cols, scratch=True):
        self.sb[name] = (Tile(slab, dtype, word_off, cols), Buf(name))
        if scratch:
            self.scratch.append(name)

    def direct(self, name, handle):
        self.sb[name] = (handle, Buf(name))

    def get(self, name):
        return self.sb[name]

    def renew(self, fence_toks, *names):
        for n in names:
            h, _ = self.sb[n]
            b = Buf(n)
            b.readers = list(fence_toks)
            self.sb[n] = (h, b)



def emit_ln_tg(ctx, src_name, dst_name):
    nc, P = ctx.nc, ctx.P
    src, src_b = ctx.get(src_name)
    dst, dst_b = ctx.get(dst_name)
    x2, x2_b = ctx.get("lnx2")
    xb, xb_b = ctx.get("lnxb")
    ceps, ceps_b = ctx.get("ceps")
    oc64, _ = ctx.get("ones_c64")
    orow, _ = ctx.get("ones_row")
    tmp, tmp_b = ctx.get("lntmp")
    lnv, lnv_b = ctx.get("lnv2")
    psA, psA_b = ctx.ps[6]
    psB, psB_b = ctx.ps[7]

    def ch(t, c):
        return t.ap().rearrange("p (c t) -> p c t", c=NC)[:, c, :]

    P.op("scalar", lambda e: e.activation(
        out=x2.ap().rearrange("p (c t) -> p c t", c=NC),
        in_=src.ap().rearrange("p (c t) -> p c t", c=NC), func=AF.Square),
        reads=[src_b], writes=[x2_b])
    P.op("vector", lambda e: e.tensor_copy(
        xb.ap().rearrange("p (c t) -> p c t", c=NC),
        src.ap().rearrange("p (c t) -> p c t", c=NC)),
        reads=[src_b], writes=[xb_b])

    def f_sums(eng):
        last = None
        for c in range(NC):
            last = eng.matmul(psA.ap()[0:64, :], oc64.ap(), ch(xb, c),
                              start=(c == 0), stop=(c == NC - 1))
        for c in range(NC):
            last = eng.matmul(psB.ap()[0:64, :], oc64.ap(), ch(x2, c),
                              start=(c == 0), stop=(c == NC - 1))
        return last
    P.op("tensor", f_sums, reads=[xb_b, x2_b], writes=[psA_b, psB_b])

    def f_stats(eng):
        eng.tensor_scalar_mul(tmp.ap()[0:64, :], psA.ap()[0:64, :], 1.0 / D)
        eng.tensor_scalar_mul(lnv.ap()[0:64, :], psB.ap()[0:64, :], 1.0 / D)
        eng.tensor_tensor(psA.ap()[0:64, :], tmp.ap()[0:64, :], tmp.ap()[0:64, :],
                          ALU.mult)
        return eng.tensor_sub(lnv.ap()[0:64, :], lnv.ap()[0:64, :], psA.ap()[0:64, :])
    P.op("vector", f_stats, reads=[psA_b, psB_b], writes=[tmp_b, lnv_b, psA_b])

    def f_rstd(eng):
        eng.activation(out=lnv.ap()[0:64, :], in_=lnv.ap()[0:64, :], func=AF.Ln,
                       bias=ceps.ap()[0:64, 0:1])
        return eng.activation(out=lnv.ap()[0:64, :], in_=lnv.ap()[0:64, :],
                              func=AF.Exp, scale=-0.5)
    P.op("scalar", f_rstd, reads=[lnv_b, ceps_b], writes=[lnv_b])

    P.op("vector", lambda e: e.scalar_tensor_tensor(
        tmp.ap()[0:64, :], tmp.ap()[0:64, :], -1.0, lnv.ap()[0:64, :],
        ALU.mult, ALU.mult), reads=[tmp_b, lnv_b], writes=[tmp_b])

    def f_bc(eng):
        eng.matmul(psB.ap()[:, :], orow.ap(), lnv.ap()[0:1, :], start=True, stop=True)
        return eng.matmul(psA.ap()[:, :], orow.ap(), tmp.ap()[0:1, :], start=True, stop=True)
    P.op("tensor", f_bc, reads=[tmp_b, lnv_b], writes=[psA_b, psB_b])

    for c in range(NC):
        def f_m(eng, c=c):
            return eng.tensor_tensor(tmp.ap(), ch(src, c), psB.ap(), ALU.mult)
        P.op("vector", f_m, reads=[src_b, psB_b, tmp_b], writes=[tmp_b])

        def f_a(eng, c=c):
            return eng.tensor_tensor(ch(dst, c), tmp.ap(), psA.ap(), ALU.add)
        P.op("vector", f_a, reads=[tmp_b, psA_b], writes=[dst_b])



def emit_load_res(ctx, src_dram, tg, dst_name):
    P = ctx.P
    dst, dst_b = ctx.get(dst_name)

    def f(e):
        return e.dma_start(
            out=dst.ap().rearrange("p (c t) -> p c t", c=NC),
            in_=src_dram.ap().rearrange("(c p) t -> p c t", p=128)[:, :, tg * 512:(tg + 1) * 512])
    P.op("sync", f, writes=[dst_b], dma=True)


def emit_store_res(ctx, dst_dram, tg, src_name):
    P = ctx.P
    src, src_b = ctx.get(src_name)

    def f(e):
        return e.dma_start(
            out=dst_dram.ap().rearrange("(c p) t -> p c t", p=128)[:, :, tg * 512:(tg + 1) * 512],
            in_=src.ap().rearrange("p (c t) -> p c t", c=NC))
    return P.op("gpsimd", f, reads=[src_b], dma=True)


def emit_load_w(ctx, dram_name, dst_name):
    P = ctx.P
    dst, dst_b = ctx.get(dst_name)
    dh = ctx.dram[dram_name]
    nchunks = dh.ap().shape[0]

    def f(e):
        return e.dma_start(out=dst.ap().rearrange("p (c d) -> p c d", c=nchunks),
                           in_=dh.ap().transpose([1, 0, 2]))
    P.op("sync", f, writes=[dst_b], dma=True)


def emit_proj_tg(ctx, src_name, dst_name, w_name, *, act=None, act_scale=1.0,
                 act_bias_name=None, psum_banks=(0, 1), dst_off=0):
    nc, P = ctx.nc, ctx.P
    src, src_b = ctx.get(src_name)
    dst, dst_b = ctx.get(dst_name)
    w, w_b = ctx.get(w_name)
    if act_bias_name is not None:
        bias, bias_b = ctx.get(act_bias_name)
    for do in range(NC):
        psh, psb = ctx.ps[psum_banks[do % len(psum_banks)]]

        def f_mm(eng, do=do, psh=psh):
            last = None
            for dc in range(NC):
                lhsT = w.ap()[:, dc * D + do * 128: dc * D + do * 128 + 128]
                rhs = src.ap().rearrange("p (c t) -> p c t", c=NC)[:, dc, :]
                last = eng.matmul(psh.ap(), lhsT, rhs,
                                  start=(dc == 0), stop=(dc == NC - 1))
            return last
        P.op("tensor", f_mm, reads=[src_b, w_b], writes=[psb])

        o_ap = dst.ap()[:, dst_off + do * 512: dst_off + do * 512 + 512]

        def f_ev(eng, do=do, psh=psh, o_ap=o_ap):
            kw = {}
            if act_bias_name is not None:
                kw["bias"] = bias.ap()[:, do:do + 1]
            return eng.activation(out=o_ap, in_=psh.ap(),
                                  func=(act or AF.Copy), scale=act_scale, **kw)
        reads = [psb] + ([bias_b] if act_bias_name is not None else [])
        P.op("scalar", f_ev, reads=reads, writes=[dst_b])



def emit_ffn0(ctx, src_dram, dst_dram):
    nc, P = ctx.nc, ctx.P
    hbuf, hbuf_b = ctx.get("hfull")

    emit_load_w(ctx, "w_ffn0_w", "wslot0")
    emit_load_w(ctx, "w_ffn0_a1", "wslot1")
    for tg in range(NTG):
        rb = f"rbuf{tg % 2}"
        emit_load_res(ctx, src_dram, tg, rb)
        emit_ln_tg(ctx, rb, "xg")
        emit_proj_tg(ctx, "xg", "hfull", "wslot0", dst_off=tg * 512 * NC)
    for tg in range(NTG):
        xg, xg_b = ctx.get("xg")

        def f_cp(eng, tg=tg):
            return eng.tensor_copy(
                xg.ap().rearrange("p (c t) -> p c t", c=NC),
                hbuf.ap()[:, tg * 512 * NC:(tg + 1) * 512 * NC].rearrange("p (c t) -> p c t", c=NC))
        P.op("vector", f_cp, reads=[hbuf_b], writes=[xg_b])
        emit_proj_tg(ctx, "xg", "hfull", "wslot1", dst_off=tg * 512 * NC)
    emit_load_w(ctx, "w_ffn0_a2", "wslot0")
    for tg in range(NTG):
        rb = f"rbuf{tg % 2}"
        emit_load_res(ctx, src_dram, tg, rb)
        xg, xg_b = ctx.get("xg")

        def f_cp(eng, tg=tg):
            return eng.tensor_copy(
                xg.ap().rearrange("p (c t) -> p c t", c=NC),
                hbuf.ap()[:, tg * 512 * NC:(tg + 1) * 512 * NC].rearrange("p (c t) -> p c t", c=NC))
        P.op("vector", f_cp, reads=[hbuf_b], writes=[xg_b])
        emit_proj_tg(ctx, "xg", "tg_out", "wslot0", act=AF.Tanh,
                     act_bias_name="b_ffn0")
        emit_add_store(ctx, dst_dram, tg, rb, "tg_out")


def emit_add_store(ctx, dst_dram, tg, rb_name, add_name):
    P = ctx.P
    rbuf, rbuf_b = ctx.get(rb_name)
    addt, addt_b = ctx.get(add_name)
    for c in range(NC):
        def f(eng, c=c):
            o = rbuf.ap().rearrange("p (c t) -> p c t", c=NC)[:, c, :]
            i = addt.ap().rearrange("p (c t) -> p c t", c=NC)[:, c, :]
            return eng.tensor_add(o, o, i)
        P.op("vector", f, reads=[addt_b, rbuf_b], writes=[rbuf_b])
    emit_store_res(ctx, dst_dram, tg, rb_name)



def emit_ffn1(ctx, src_dram, dst_dram):
    nc, P = ctx.nc, ctx.P
    k2, k2_b = ctx.get("hfull")
    mur, mur_b = ctx.get("v_mu_r")
    muk, muk_b = ctx.get("v_mu_k")
    carry, carry_b = ctx.get("carry8")
    tmp, tmp_b = ctx.get("lntmp")

    def mix_tg(tg, mu_t, mu_b, out_name):
        xg, xg_b = ctx.get("xg")
        xp, xp_b = ctx.get("xp")
        out, out_b = ctx.get(out_name)
        for c in range(NC):
            def f_shift(eng, c=c, tg=tg):
                o = xp.ap().rearrange("p (c t) -> p c t", c=NC)
                i = xg.ap().rearrange("p (c t) -> p c t", c=NC)
                eng.tensor_copy(o[:, c, 1:512], i[:, c, 0:511])
                if tg == 0:
                    return eng.memset(o[:, c, 0:1], 0.0)
                return eng.tensor_copy(o[:, c, 0:1], carry.ap()[:, c:c + 1])
            P.op("vector", f_shift, reads=[xg_b, carry_b], writes=[xp_b])
        def f_carry(eng, tg=tg):
            last = None
            for c in range(NC):
                i = xg.ap().rearrange("p (c t) -> p c t", c=NC)[:, c, 511:512]
                last = eng.tensor_copy(carry.ap()[:, c:c + 1], i)
            return last
        P.op("gpsimd", f_carry, reads=[xg_b], writes=[carry_b])
        for c in range(NC):
            def f_d(eng, c=c):
                return eng.tensor_sub(
                    tmp.ap(),
                    xg.ap().rearrange("p (c t) -> p c t", c=NC)[:, c, :],
                    xp.ap().rearrange("p (c t) -> p c t", c=NC)[:, c, :])
            P.op("vector", f_d, reads=[xg_b, xp_b], writes=[tmp_b])

            def f_mx(eng, c=c):
                return eng.scalar_tensor_tensor(
                    out.ap().rearrange("p (c t) -> p c t", c=NC)[:, c, :],
                    tmp.ap(), mu_t.ap()[:, c:c + 1],
                    xp.ap().rearrange("p (c t) -> p c t", c=NC)[:, c, :],
                    ALU.mult, ALU.add)
            P.op("vector", f_mx, reads=[tmp_b, xp_b, mu_b], writes=[out_b])

    emit_load_w(ctx, "w_ffn1_k", "wslot0")
    for tg in range(NTG):
        rb = f"rbuf{tg % 2}"
        emit_load_res(ctx, src_dram, tg, rb)
        emit_ln_tg(ctx, rb, "xg")
        mix_tg(tg, muk, muk_b, "mixg")
        mixg, mixg_b = ctx.get("mixg")
        for do in range(NC):
            psh, psb = ctx.ps[do % 2]

            def f_mm(eng, do=do, psh=psh):
                last = None
                for dc in range(NC):
                    w, w_b = ctx.get("wslot0")
                    lhsT = w.ap()[:, dc * D + do * 128: dc * D + do * 128 + 128]
                    rhs = mixg.ap().rearrange("p (c t) -> p c t", c=NC)[:, dc, :]
                    last = eng.matmul(psh.ap(), lhsT, rhs,
                                      start=(dc == 0), stop=(dc == NC - 1))
                return last
            P.op("tensor", f_mm, reads=[mixg_b, ctx.get("wslot0")[1]], writes=[psb])

            def f_rl(eng, do=do, psh=psh, tg=tg):
                return eng.activation(out=tmp.ap(), in_=psh.ap(), func=AF.Relu)
            P.op("scalar", f_rl, reads=[psb], writes=[tmp_b])

            def f_sq(eng, do=do, tg=tg):
                o = k2.ap()[:, tg * 512 * NC + do * 512: tg * 512 * NC + do * 512 + 512]
                return eng.tensor_tensor(o, tmp.ap(), tmp.ap(), ALU.mult)
            P.op("vector", f_sq, reads=[tmp_b], writes=[k2_b])

    emit_load_w(ctx, "w_ffn1_r", "wslot1")
    emit_load_w(ctx, "w_ffn1_v", "wslot0")
    carry2, carry2_b = ctx.get("carry8")
    P.op("vector", lambda e: e.memset(carry2.ap(), 0.0), writes=[carry2_b])
    for tg in range(NTG):
        rb = f"rbuf{tg % 2}"
        emit_load_res(ctx, src_dram, tg, rb)
        emit_ln_tg(ctx, rb, "xg")
        mix_tg(tg, mur, mur_b, "mixg")
        mixg, mixg_b = ctx.get("mixg")
        rt, rt_b = ctx.get("xp")
        for do in range(NC):
            psh, psb = ctx.ps[do % 2]

            def f_mm(eng, do=do, psh=psh):
                last = None
                for dc in range(NC):
                    w, w_b = ctx.get("wslot1")
                    lhsT = w.ap()[:, dc * D + do * 128: dc * D + do * 128 + 128]
                    rhs = mixg.ap().rearrange("p (c t) -> p c t", c=NC)[:, dc, :]
                    last = eng.matmul(psh.ap(), lhsT, rhs,
                                      start=(dc == 0), stop=(dc == NC - 1))
                return last
            P.op("tensor", f_mm, reads=[mixg_b, ctx.get("wslot1")[1]], writes=[psb])

            def f_sig(eng, do=do, psh=psh):
                o = rt.ap().rearrange("p (c t) -> p c t", c=NC)[:, do, :]
                return eng.activation(out=o, in_=psh.ap(), func=AF.Sigmoid)
            P.op("scalar", f_sig, reads=[psb], writes=[rt_b])
        outg, outg_b = ctx.get("tg_out")
        for do in range(NC):
            psh, psb = ctx.ps[2 + do % 2]

            def f_mm(eng, do=do, psh=psh, tg=tg):
                last = None
                for dc in range(NC):
                    w, w_b = ctx.get("wslot0")
                    lhsT = w.ap()[:, dc * D + do * 128: dc * D + do * 128 + 128]
                    rhs = k2.ap()[:, tg * 512 * NC + dc * 512: tg * 512 * NC + dc * 512 + 512]
                    last = eng.matmul(psh.ap(), lhsT, rhs,
                                      start=(dc == 0), stop=(dc == NC - 1))
                return last
            P.op("tensor", f_mm, reads=[k2_b, ctx.get("wslot0")[1]], writes=[psb])

            def f_mul(eng, do=do, psh=psh):
                o = outg.ap().rearrange("p (c t) -> p c t", c=NC)[:, do, :]
                return eng.tensor_tensor(o, psh.ap(),
                                         rt.ap().rearrange("p (c t) -> p c t", c=NC)[:, do, :],
                                         ALU.mult)
            P.op("vector", f_mul, reads=[psb, rt_b], writes=[outg_b])
        emit_add_store(ctx, dst_dram, tg, rb, "tg_out")



CA0 = 128
A0EPS = 1e-6


def emit_a0_tables(ctx):
    nc, P = ctx.nc, ctx.P
    tab, tab_b = ctx.get("a0_tab")
    mdec, mdec_b = ctx.get("a0_mdec")
    qdecb, qdecb_b = ctx.get("a0_qdecb")
    kdec, kdec_b = ctx.get("a0_kdec")
    ident, ident_b = ctx.get("ident")
    cmask, cmask_b = ctx.get("a0_cmask")
    gam = ctx.dram["a0_gam"]
    gam_sb, gam_b = ctx.get("a0_gamsb")
    ones_row, or_b = ctx.get("ones_row")
    psA, psA_b = ctx.ps[6]
    ALP = nc.allow_low_precision

    P.op("sync", lambda e: e.dma_start(out=gam_sb.ap()[0:1, 0:2], in_=gam.ap()),
         writes=[gam_b], dma=True)
    P.op("sync", lambda e: e.dma_start(out=cmask.ap(), in_=ctx.dram["cmaskTd"].ap()),
         writes=[cmask_b], dma=True)

    P.op("scalar", lambda e: e.activation(out=tab.ap()[0:1, 0:1],
                                          in_=gam_sb.ap()[0:1, 0:1], func=AF.Abs),
         reads=[gam_b], writes=[tab_b])

    def f_lng(eng):
        eng.activation(out=tab.ap()[0:1, 1:2], in_=tab.ap()[0:1, 0:1], func=AF.Ln)
        eng.activation(out=tab.ap()[0:1, 2:3], in_=tab.ap()[0:1, 1:2], func=AF.Exp,
                       scale=float(CA0))
        return eng.sign(tab.ap()[0:1, 3:4], gam_sb.ap()[0:1, 0:1])
    P.op("scalar", f_lng, reads=[tab_b, gam_b], writes=[tab_b])

    P.op("vector", lambda e: e.memset(tab.ap()[0:1, 4:5], 1.0), writes=[tab_b])

    def f_bc_scal(eng):
        return eng.matmul(psA.ap()[:, 0:4], ones_row.ap(), tab.ap()[0:1, 0:4],
                          start=True, stop=True)
    P.op("tensor", f_bc_scal, reads=[tab_b], writes=[psA_b])

    def f_cp_scal(eng):
        e = eng
        e.copy(tab.ap()[:, 8:9], psA.ap()[:, 1:2])
        e.copy(tab.ap()[:, 9:10], psA.ap()[:, 2:3])
        e.copy(kdec.ap()[:, 1:2], psA.ap()[:, 2:3])
        return e.copy(tab.ap()[:, 10:11], psA.ap()[:, 3:4])
    P.op("scalar", f_cp_scal, reads=[psA_b], writes=[tab_b, kdec_b])

    def f_par(eng):
        return eng.iota(tab.ap()[0:1, 640:768].rearrange("p (a b) -> p a b", a=64),
                        [[0, 64], [1, 2]], base=0, channel_multiplier=0,
                        allow_small_or_imprecise_dtypes=True)
    P.op("gpsimd", f_par, writes=[tab_b])

    def f_sgnm1(eng):
        return eng.activation(out=tab.ap()[0:1, 5:6], in_=tab.ap()[0:1, 3:4],
                              func=AF.Copy, bias=-1.0)
    P.op("scalar", f_sgnm1, reads=[tab_b], writes=[tab_b])

    def f_sgnrows(eng):
        eng.tensor_scalar(tab.ap()[0:1, 768:896], tab.ap()[0:1, 640:768],
                          tab.ap()[0:1, 5:6], 1.0, ALU.mult, ALU.add)
        eng.tensor_scalar(tab.ap()[0:1, 896:1024], tab.ap()[0:1, 640:768],
                          -1.0, 1.0, ALU.mult, ALU.add)
        return eng.tensor_scalar(tab.ap()[0:1, 896:1024], tab.ap()[0:1, 896:1024],
                                 tab.ap()[0:1, 5:6], 1.0, ALU.mult, ALU.add)
    P.op("vector", f_sgnrows, reads=[tab_b], writes=[tab_b])

    def f_sgncol(eng):
        return eng.matmul(psA.ap()[:, 4:5], tab.ap()[0:1, 896:1024],
                          tab.ap()[0:1, 4:5], start=True, stop=True)
    P.op("tensor", f_sgncol, reads=[tab_b], writes=[psA_b])

    P.op("vector", lambda e: e.tensor_copy(tab.ap()[:, 11:12], psA.ap()[:, 4:5]),
         reads=[psA_b], writes=[tab_b])

    def f_mdiota(eng):
        return eng.iota(tab.ap()[:, 256:384], [[1, 128]], base=0,
                        channel_multiplier=-1, allow_small_or_imprecise_dtypes=True)
    P.op("gpsimd", f_mdiota, writes=[tab_b])

    P.op("vector", lambda e: e.tensor_scalar_mul(tab.ap()[:, 256:384],
                                                 tab.ap()[:, 256:384],
                                                 tab.ap()[:, 8:9]),
         reads=[tab_b], writes=[tab_b])
    P.op("scalar", lambda e: e.activation(out=tab.ap()[:, 256:384],
                                          in_=tab.ap()[:, 256:384], func=AF.Exp),
         reads=[tab_b], writes=[tab_b])

    def f_sgnouter(eng):
        return eng.matmul(psA.ap()[:, 128:256], tab.ap()[0:1, 768:896],
                          tab.ap()[0:1, 768:896], start=True, stop=True)
    P.op("tensor", f_sgnouter, reads=[tab_b], writes=[psA_b])

    def f_mdec(eng):
        eng.tensor_tensor(tab.ap()[:, 384:512], tab.ap()[:, 256:384],
                          psA.ap()[:, 128:256], ALU.mult)
        with ALP(reason="bf16 decay mask"):
            return eng.tensor_tensor(mdec.ap(), tab.ap()[:, 384:512], cmask.ap(),
                                     ALU.mult)
    P.op("vector", f_mdec, reads=[tab_b, psA_b, cmask_b], writes=[mdec_b, tab_b])

    def f_kdiota(eng):
        return eng.iota(tab.ap()[:, 12:13], [[0, 1]], base=CA0 - 1,
                        channel_multiplier=-1, allow_small_or_imprecise_dtypes=True)
    P.op("gpsimd", f_kdiota, writes=[tab_b])
    P.op("vector", lambda e: e.tensor_scalar_mul(tab.ap()[:, 12:13],
                                                 tab.ap()[:, 12:13], tab.ap()[:, 8:9]),
         reads=[tab_b], writes=[tab_b])
    P.op("scalar", lambda e: e.activation(out=tab.ap()[:, 13:14],
                                          in_=tab.ap()[:, 12:13], func=AF.Exp),
         reads=[tab_b], writes=[tab_b])
    P.op("vector", lambda e: e.tensor_tensor(tab.ap()[:, 14:15], tab.ap()[:, 13:14],
                                             tab.ap()[:, 11:12], ALU.mult),
         reads=[tab_b], writes=[tab_b])
    P.op("scalar", lambda e: e.copy(kdec.ap()[:, 0:1], tab.ap()[:, 14:15]),
         reads=[tab_b], writes=[kdec_b])

    def f_qiota(eng):
        return eng.iota(tab.ap()[0:1, 128:256], [[1, 128]], base=1,
                        channel_multiplier=0, allow_small_or_imprecise_dtypes=True)
    P.op("gpsimd", f_qiota, writes=[tab_b])
    P.op("vector", lambda e: e.tensor_scalar_mul(tab.ap()[0:1, 128:256],
                                                 tab.ap()[0:1, 128:256],
                                                 tab.ap()[0:1, 1:2]),
         reads=[tab_b], writes=[tab_b])
    P.op("scalar", lambda e: e.activation(out=tab.ap()[0:1, 128:256],
                                          in_=tab.ap()[0:1, 128:256], func=AF.Exp),
         reads=[tab_b], writes=[tab_b])
    P.op("vector", lambda e: e.tensor_tensor(tab.ap()[0:1, 128:256],
                                             tab.ap()[0:1, 128:256],
                                             tab.ap()[0:1, 896:1024], ALU.mult),
         reads=[tab_b], writes=[tab_b])

    def f_qdbc(eng):
        return eng.matmul(psA.ap()[:, 128:256], ones_row.ap(),
                          tab.ap()[0:1, 128:256], start=True, stop=True)
    P.op("tensor", f_qdbc, reads=[tab_b], writes=[psA_b])

    def f_qdcp(eng):
        with ALP(reason="bf16 decay table"):
            return eng.tensor_copy(qdecb.ap(), psA.ap()[:, 128:256])
    P.op("vector", f_qdcp, reads=[psA_b], writes=[qdecb_b])


def emit_a0(ctx, src_dram, dst_dram):
    nc, P = ctx.nc, ctx.P
    grp, grp_b = ctx.get("a0_grp")
    S, S_b = ctx.get("a0_S")
    z, z_b = ctx.get("a0_z")
    zbf, zbf_b = ctx.get("a0_zbf")
    mdec, mdec_b = ctx.get("a0_mdec")
    qdecb, qdecb_b = ctx.get("a0_qdecb")
    kdec, kdec_b = ctx.get("a0_kdec")
    bv_bc, bv_bc_b = ctx.get("a0_bvbc")
    ident, ident_b = ctx.get("ident")
    ktm, ktm_b = ctx.get("a0_ktm")
    bq, bq_b = ctx.get("a0_bqsb")
    bk, bk_b = ctx.get("a0_bksb")
    ones_row, _ = ctx.get("ones_row")
    ones_colb, _ = ctx.get("ones_colb")
    tmp, tmp_b = ctx.get("lntmp")
    xlng, xlng_b = ctx.get("xg")
    ALP = nc.allow_low_precision

    emit_a0_tables(ctx)

    P.op("sync", lambda e: e.dma_start(out=bq.ap(), in_=ctx.dram["a0_bqd"].ap()),
         writes=[bq_b], dma=True)
    P.op("sync", lambda e: e.dma_start(out=bk.ap(), in_=ctx.dram["a0_bkd"].ap()),
         writes=[bk_b], dma=True)
    emit_load_w(ctx, "w_a0_q", "wslot0")
    emit_load_w(ctx, "w_a0_k", "wslot1")
    emit_load_w(ctx, "w_a0_v", "wslot2")

    bvr, bvr_b = ctx.get("a0_bvrsb")
    P.op("sync", lambda e: e.dma_start(out=bvr.ap()[0:1, :], in_=ctx.dram["a0_bvrd"].ap()),
         writes=[bvr_b], dma=True)
    psB, psB_b = ctx.ps[7]
    for half in range(2):
        def f_bvb(eng, half=half):
            return eng.matmul(psB.ap(), ones_row.ap(),
                              bvr.ap()[0:1, half * 512:(half + 1) * 512],
                              start=True, stop=True)
        P.op("tensor", f_bvb, reads=[bvr_b], writes=[psB_b])

        def f_bvc(eng, half=half):
            with ALP(reason="bf16 bv"):
                return eng.tensor_copy(bv_bc.ap()[:, half * 512:(half + 1) * 512],
                                       psB.ap())
        P.op("vector", f_bvc, reads=[psB_b], writes=[bv_bc_b])

    P.op("vector", lambda e: e.memset(S.ap(), 0.0), writes=[S_b])
    P.op("vector", lambda e: e.memset(z.ap(), 0.0), writes=[z_b])

    wq, wq_b0 = ctx.get("wslot0")
    wk, wk_b0 = ctx.get("wslot1")
    wv, wv_b0 = ctx.get("wslot2")

    for tg in range(NTG):
        rb = f"rbuf{tg % 2}"
        emit_load_res(ctx, src_dram, tg, rb)
        rbuf, rbuf_b = ctx.get(rb)
        emit_ln_tg(ctx, rb, "xg")

        for wi, (wob, wbb, bb, bbuf, off) in enumerate(
                ((wq, ctx.get("wslot0")[1], bq, bq_b, 0),
                 (wk, ctx.get("wslot1")[1], bk, bk_b, 4096))):
            for do in range(NC):
                psp, psp_b = ctx.ps[do % 2]

                def f_mm(eng, wob=wob, do=do, psp=psp):
                    last = None
                    for dc in range(NC):
                        lhsT = wob.ap()[:, dc * D + do * 128: dc * D + do * 128 + 128]
                        rhs = xlng.ap().rearrange("p (c t) -> p c t", c=NC)[:, dc, :]
                        last = eng.matmul(psp.ap(), lhsT, rhs,
                                          start=(dc == 0), stop=(dc == NC - 1))
                    return last
                P.op("tensor", f_mm, reads=[xlng_b, wbb], writes=[psp_b])

                def f_ev(eng, bb=bb, do=do, off=off, psp=psp):
                    o = grp.ap()[:, off + do * 512: off + do * 512 + 512]
                    return eng.activation(out=o, in_=psp.ap(), func=AF.Relu,
                                          bias=bb.ap()[:, do:do + 1])
                P.op("scalar", f_ev, reads=[psp_b, bbuf], writes=[grp_b])

        for ci in range(4):
            for eh in range(2):
                psv, psv_b = ctx.ps[2 + (ci * 2 + eh) % 2]

                def f_vmm(eng, ci=ci, eh=eh, psv=psv):
                    last = None
                    for dc in range(NC):
                        lhsT = xlng.ap().rearrange("p (c t) -> p c t", c=NC)[:, dc, ci * 128:ci * 128 + 128]
                        rhs = wv.ap()[:, dc * D + eh * 512: dc * D + eh * 512 + 512]
                        last = eng.matmul(psv.ap(), lhsT, rhs,
                                          start=(dc == 0), stop=(dc == NC - 1))
                    return last
                P.op("tensor", f_vmm, reads=[xlng_b, ctx.get("wslot2")[1]], writes=[psv_b])

                def f_vev(eng, ci=ci, eh=eh, psv=psv):
                    o = grp.ap()[:, 8192 + ci * 1024 + eh * 512: 8192 + ci * 1024 + eh * 512 + 512]
                    with ALP(reason="bf16 v evac"):
                        return eng.tensor_tensor(o, psv.ap(),
                                                 bv_bc.ap()[:, eh * 512:(eh + 1) * 512],
                                                 ALU.add)
                P.op("vector", f_vev, reads=[psv_b, bv_bc_b], writes=[grp_b])

        for ci in range(4):
            def qt_ap(dc, ci=ci):
                return grp.ap()[:, dc * 512 + ci * 128: dc * 512 + ci * 128 + 128]

            def kt_ap(dc, ci=ci):
                return grp.ap()[:, 4096 + dc * 512 + ci * 128: 4096 + dc * 512 + ci * 128 + 128]

            def vtm_ap(et, ci=ci):
                return grp.ap()[:, 8192 + ci * 1024 + et * 128: 8192 + ci * 1024 + et * 128 + 128]

            psat, psat_b = ctx.ps[0]

            def f_at(eng, kt_ap=kt_ap, qt_ap=qt_ap):
                last = None
                for dc in range(NC):
                    last = eng.matmul(psat.ap()[:, 0:128], kt_ap(dc), qt_ap(dc),
                                      start=(dc == 0), stop=(dc == NC - 1))
                return last
            P.op("tensor", f_at, reads=[grp_b], writes=[psat_b])

            ab, ab_b = ctx.get("lnx2")

            def f_ab(eng):
                with ALP(reason="bf16 scores"):
                    return eng.tensor_tensor(ab.ap()[:, 0:128], psat.ap()[:, 0:128],
                                             mdec.ap(), ALU.mult)
            P.op("vector", f_ab, reads=[psat_b, mdec_b], writes=[ab_b])

            qd, qd_b = ctx.get("lnxb")

            def f_qt(eng, qt_ap=qt_ap):
                last = None
                for dc in range(NC):
                    last = eng.tensor_tensor(qd.ap()[:, dc * 128:dc * 128 + 128],
                                             qt_ap(dc), qdecb.ap(), ALU.mult)
                return last
            P.op("vector", f_qt, reads=[grp_b, qdecb_b], writes=[qd_b])

            def f_zbf(eng):
                with ALP(reason="bf16 z"):
                    return eng.tensor_copy(zbf.ap(), z.ap())
            P.op("vector", f_zbf, reads=[z_b], writes=[zbf_b])

            psd, psd_b = ctx.ps[3]

            def f_den(eng):
                eng.matmul(psd.ap()[0:1, 0:128], ones_colb.ap(), ab.ap()[:, 0:128],
                           start=True, stop=False)
                last = None
                for dc in range(NC):
                    last = eng.matmul(psd.ap()[0:1, 0:128], zbf.ap()[:, dc:dc + 1],
                                      qd.ap()[:, dc * 128:dc * 128 + 128],
                                      start=False, stop=(dc == NC - 1))
                return last
            P.op("tensor", f_den, reads=[ab_b, qd_b, zbf_b], writes=[psd_b])

            def f_rec(eng):
                eng.tensor_scalar_add(tmp.ap()[0:1, 0:128], psd.ap()[0:1, 0:128], A0EPS)
                return eng.reciprocal(tmp.ap()[0:1, 128:256], tmp.ap()[0:1, 0:128])
            P.op("vector", f_rec, reads=[psd_b], writes=[tmp_b])

            psr, psr_b = ctx.ps[4]

            def f_rbc(eng):
                return eng.matmul(psr.ap()[:, 0:128], ones_row.ap(),
                                  tmp.ap()[0:1, 128:256], start=True, stop=True)
            P.op("tensor", f_rbc, reads=[tmp_b], writes=[psr_b])

            def f_abr(eng):
                with ALP(reason="bf16 scores"):
                    return eng.tensor_tensor(ab.ap()[:, 0:128], ab.ap()[:, 0:128],
                                             psr.ap()[:, 0:128], ALU.mult)
            P.op("vector", f_abr, reads=[ab_b, psr_b], writes=[ab_b])

            def f_qdr(eng):
                last = None
                for dc in range(NC):
                    with ALP(reason="bf16 q"):
                        last = eng.tensor_tensor(qd.ap()[:, dc * 128:dc * 128 + 128],
                                                 qd.ap()[:, dc * 128:dc * 128 + 128],
                                                 psr.ap()[:, 0:128], ALU.mult)
                return last
            P.op("vector", f_qdr, reads=[qd_b, psr_b], writes=[qd_b])

            for ebank in range(2):
                psn, psn_b = ctx.ps[(1, 2)[ebank]]

                def f_num(eng, ebank=ebank, psn=psn, vtm_ap=vtm_ap):
                    last = None
                    for et4 in range(4):
                        et = ebank * 4 + et4
                        o = psn.ap()[:, et4 * 128:et4 * 128 + 128]
                        eng.matmul(o, vtm_ap(et), ab.ap()[:, 0:128], start=True, stop=False)
                        for dc in range(NC):
                            lhsT = S.ap().rearrange("p (c t) -> p c t", c=NC)[:, dc, et * 128:et * 128 + 128]
                            last = eng.matmul(o, lhsT, qd.ap()[:, dc * 128:dc * 128 + 128],
                                              start=False, stop=(dc == NC - 1))
                    return last
                P.op("tensor", f_num, reads=[grp_b, ab_b, qd_b, S_b], writes=[psn_b])

                def f_oadd(eng, ebank=ebank, psn=psn, ci=ci, rbuf=rbuf):
                    last = None
                    for et4 in range(4):
                        et = ebank * 4 + et4
                        o = rbuf.ap().rearrange("p (c t) -> p c t", c=NC)[:, et, ci * 128:ci * 128 + 128]
                        last = eng.tensor_add(o, o, psn.ap()[:, et4 * 128:et4 * 128 + 128])
                    return last
                P.op("vector", f_oadd, reads=[psn_b, rbuf_b], writes=[rbuf_b])

            pst, pst_b = ctx.ps[5]

            def f_ktr(eng, kt_ap=kt_ap):
                last = None
                for dc in range(NC):
                    last = eng.transpose(pst.ap().bitcast(BF16)[:, dc * 128:dc * 128 + 128],
                                         kt_ap(dc), ident.ap())
                return last
            P.op("tensor", f_ktr, reads=[grp_b, ident_b], writes=[pst_b])

            def f_ktv(eng):
                last = None
                for dc in range(NC):
                    last = eng.tensor_scalar_mul(ktm.ap()[:, dc * 128:dc * 128 + 128],
                                                 pst.ap().bitcast(BF16)[:, dc * 128:dc * 128 + 128],
                                                 kdec.ap()[:, 0:1])
                return last
            P.op("vector", f_ktv, reads=[pst_b, kdec_b], writes=[ktm_b])

            psz, psz_b = ctx.ps[6]

            def f_zmm(eng):
                last = None
                for dt in range(NC):
                    last = eng.matmul(psz.ap()[:, dt:dt + 1], ktm.ap()[:, dt * 128:dt * 128 + 128],
                                      ones_colb.ap(), start=True, stop=True)
                return last
            P.op("tensor", f_zmm, reads=[ktm_b], writes=[psz_b])

            def f_zup(eng):
                return eng.scalar_tensor_tensor(z.ap(), z.ap(),
                                                kdec.ap()[:, 1:2], psz.ap()[:, 0:NC],
                                                ALU.mult, ALU.add)
            P.op("vector", f_zup, reads=[z_b, kdec_b, psz_b], writes=[z_b])

            for dt in range(NC):
                for eh in range(2):
                    pss, pss_b = ctx.ps[(5, 7)[eh]]

                    def f_dsmm(eng, dt=dt, eh=eh, pss=pss, ci=ci):
                        return eng.matmul(pss.ap(), ktm.ap()[:, dt * 128:dt * 128 + 128],
                                          grp.ap()[:, 8192 + ci * 1024 + eh * 512: 8192 + ci * 1024 + eh * 512 + 512],
                                          start=True, stop=True)
                    P.op("tensor", f_dsmm, reads=[ktm_b, grp_b], writes=[pss_b])

                    def f_sup(eng, dt=dt, eh=eh, pss=pss):
                        o = S.ap().rearrange("p (c t) -> p c t", c=NC)[:, dt, eh * 512:(eh + 1) * 512]
                        with ALP(reason="bf16 state"):
                            return eng.scalar_tensor_tensor(o, o, kdec.ap()[:, 1:2],
                                                            pss.ap(), ALU.mult, ALU.add)
                    P.op("vector", f_sup, reads=[S_b, kdec_b, pss_b], writes=[S_b])

        emit_store_res(ctx, dst_dram, tg, rb)



A1_MASK_MS = (-384, -256, -128, 0)
B0_MASK_MS = (-384, -256, -128, 0, 128, 256)


def emit_mha_headln(ctx, buf_name, gcol, bcol, n_tg=NTG):
    nc, P = ctx.nc, ctx.P
    buf, buf_b = ctx.get(buf_name)
    e64, e64_b = ctx.get("e64")
    E2, E2_b = ctx.get("E2t")
    gqb, gqb_b = ctx.get("gqb")
    ceps, ceps_b = ctx.get("ceps")
    hla, hla_b = ctx.get("hl_a")
    hlb, hlb_b = ctx.get("hl_b")
    hsq, hsq_b = ctx.get("hl_sq")
    tmp, tmp_b = ctx.get("lntmp")
    ALP = nc.allow_low_precision
    for tg in range(n_tg):
        psA, psA_b = ctx.ps[2]
        psB, psB_b = ctx.ps[3]

        def bsl(tg=tg):
            return buf.ap()[:, tg * 512:(tg + 1) * 512]

        P.op("scalar", lambda e, tg=tg: e.activation(out=hsq.ap(), in_=bsl(tg),
                                                     func=AF.Square),
             reads=[buf_b], writes=[hsq_b])

        def f_sums(eng, tg=tg, psA=psA, psB=psB):
            eng.matmul(psA.ap()[0:2, :], e64.ap(), bsl(tg), start=True, stop=True)
            return eng.matmul(psB.ap()[0:2, :], e64.ap(), hsq.ap(), start=True, stop=True)
        P.op("tensor", f_sums, reads=[buf_b, hsq_b, e64_b], writes=[psA_b, psB_b])

        def f_means(eng, psA=psA, psB=psB):
            eng.tensor_scalar_mul(hla.ap()[0:2, :], psA.ap()[0:2, :], 1.0 / 64)
            return eng.tensor_scalar_mul(hlb.ap()[0:2, :], psB.ap()[0:2, :], 1.0 / 64)
        P.op("vector", f_means, reads=[psA_b, psB_b], writes=[hla_b, hlb_b])

        def f_var(eng):
            eng.tensor_tensor(tmp.ap()[0:2, :], hla.ap()[0:2, :], hla.ap()[0:2, :],
                              ALU.mult)
            return eng.tensor_sub(hlb.ap()[0:2, :], hlb.ap()[0:2, :], tmp.ap()[0:2, :])
        P.op("vector", f_var, reads=[hla_b, hlb_b], writes=[hlb_b, tmp_b])

        def f_rstd(eng):
            eng.activation(out=hlb.ap()[0:2, :], in_=hlb.ap()[0:2, :], func=AF.Ln,
                           bias=ceps.ap()[0:2, 0:1])
            return eng.activation(out=hlb.ap()[0:2, :], in_=hlb.ap()[0:2, :],
                                  func=AF.Exp, scale=-0.5)
        P.op("scalar", f_rstd, reads=[hlb_b, ceps_b], writes=[hlb_b])

        P.op("vector", lambda e: e.scalar_tensor_tensor(
            hla.ap()[0:2, :], hla.ap()[0:2, :], -1.0, hlb.ap()[0:2, :],
            ALU.mult, ALU.mult), reads=[hla_b, hlb_b], writes=[hla_b])

        gb, gb_b = ctx.get("hl_gb")

        def f_cast(eng):
            with ALP(reason="bf16 headln rows"):
                eng.tensor_copy(gb.ap()[0:2, 0:512], hlb.ap()[0:2, :])
                return eng.tensor_copy(gb.ap()[0:2, 512:1024], hla.ap()[0:2, :])
        P.op("vector", f_cast, reads=[hla_b, hlb_b], writes=[gb_b])

        def f_bc(eng, psA=psA, psB=psB):
            eng.matmul(psA.ap(), E2.ap()[0:2, :], gb.ap()[0:2, 0:512],
                       start=True, stop=True)
            return eng.matmul(psB.ap(), E2.ap()[0:2, :], gb.ap()[0:2, 512:1024],
                              start=True, stop=True)
        P.op("tensor", f_bc, reads=[gb_b, E2_b], writes=[psA_b, psB_b])

        def f_ap1(eng, tg=tg, psA=psA):
            with ALP(reason="bf16 headln"):
                return eng.tensor_tensor(hsq.ap(), bsl(tg), psA.ap(), ALU.mult)
        P.op("vector", f_ap1, reads=[buf_b, psA_b], writes=[hsq_b])

        def f_ap2(eng, psB=psB):
            with ALP(reason="bf16 headln"):
                return eng.tensor_tensor(hsq.ap(), hsq.ap(), psB.ap(), ALU.add)
        P.op("vector", f_ap2, reads=[hsq_b, psB_b], writes=[hsq_b])

        def f_ap3(eng, tg=tg):
            with ALP(reason="bf16 headln"):
                return eng.tensor_scalar(bsl(tg), hsq.ap(),
                                         gqb.ap()[:, gcol:gcol + 1],
                                         gqb.ap()[:, bcol:bcol + 1],
                                         ALU.mult, ALU.add)
        P.op("vector", f_ap3, reads=[hsq_b, gqb_b], writes=[buf_b])


def emit_mha(ctx, src_dram, dst_dram, kind):
    nc, P = ctx.nc, ctx.P
    is_b0 = kind == "b0"
    expb = EXPB_B0 if is_b0 else EXPB_A1
    q_scale = 0.125 if is_b0 else 1.0
    ident, ident_b = ctx.get("ident")
    e64sel, e64sel_b = ctx.get("e64sel")
    tmp, tmp_b = ctx.get("lntmp")
    ALP = nc.allow_low_precision

    msk, msk_b = ctx.get("mskt")
    P.op("sync", lambda e: e.dma_start(out=msk.ap(), in_=ctx.dram[f"msks_{kind}"].ap()),
         writes=[msk_b], dma=True)
    for tg in range(NTG):
        rb = f"rbuf{tg % 2}"
        emit_load_res(ctx, src_dram, tg, rb)
        emit_ln_tg(ctx, rb, f"xln{tg}")

    def pair_block(hp, slot):
        vt, vt_b = ctx.get("vt")
        qt, qt_b = ctx.get("qt")
        kt, kt_b = ctx.get("kt")
        ot, ot_b = ctx.get("otile")
        pball = [ctx.get(f"pb{i}") for i in range(8)]
        wp, wp_b = ctx.get("wpair")
        for wi, dname in enumerate(("w_q", "w_k", "w_v")):
            def f_ldw(e, wi=wi, dname=dname, hp=hp):
                src = ctx.dram[f"{kind}_{dname}"].ap()[:, :, hp * 128:(hp + 1) * 128]
                return e.dma_start(
                    out=wp.ap()[:, wi * NC * 128:(wi + 1) * NC * 128].rearrange("p (c d) -> p c d", c=NC),
                    in_=src.transpose([1, 0, 2]))
            P.op("sync", f_ldw, writes=[wp_b], dma=True)

        def wsl(wi, dc):
            return wp.ap()[:, wi * NC * 128 + dc * 128: wi * NC * 128 + dc * 128 + 128]

        for wi, (dstt, dstb) in enumerate(((qt, ctx.get("qt")[1]), (kt, ctx.get("kt")[1]))):
            for tg in range(NTG):
                psp, psp_b = ctx.ps[tg % 2]
                xl, xl_b = ctx.get(f"xln{tg}")

                def f_mm(eng, wi=wi, tg=tg, psp=psp, xl=xl):
                    last = None
                    for dc in range(NC):
                        last = eng.matmul(psp.ap(), wsl(wi, dc),
                                          xl.ap().rearrange("p (c t) -> p c t", c=NC)[:, dc, :],
                                          start=(dc == 0), stop=(dc == NC - 1))
                    return last
                P.op("tensor", f_mm, reads=[xl_b, wp_b], writes=[psp_b])

                def f_ev(eng, tg=tg, psp=psp, dstt=dstt):
                    return eng.activation(out=dstt.ap()[:, tg * 512:(tg + 1) * 512],
                                          in_=psp.ap(), func=AF.Copy)
                P.op("scalar", f_ev, reads=[psp_b], writes=[dstb])

        if not is_b0:
            emit_mha_headln(ctx, "qt", 0, 1)
            emit_mha_headln(ctx, "kt", 2, 3)

        P.op("vector", lambda e: e.memset(vt.ap(), 1.0), writes=[vt_b])
        vf, vf_b = pball[0]
        for tg in range(NTG):
            psp, psp_b = ctx.ps[2]
            xl, xl_b = ctx.get(f"xln{tg}")

            def f_vmm(eng, tg=tg, psp=psp, xl=xl):
                last = None
                for dc in range(NC):
                    last = eng.matmul(psp.ap(), wsl(2, dc),
                                      xl.ap().rearrange("p (c t) -> p c t", c=NC)[:, dc, :],
                                      start=(dc == 0), stop=(dc == NC - 1))
                return last
            P.op("tensor", f_vmm, reads=[xl_b, wp_b], writes=[psp_b])

            def f_vev(eng, psp=psp):
                return eng.tensor_copy(vf.ap(), psp.ap())
            P.op("vector", f_vev, reads=[psp_b], writes=[vf_b])

            pst, pst_b = ctx.ps[3]
            for c4 in range(4):
                def f_vtr(eng, c4=c4, pst=pst):
                    return eng.transpose(pst.ap().bitcast(BF16)[:, 0:128],
                                         vf.ap()[:, c4 * 128:c4 * 128 + 128], ident.ap())
                P.op("tensor", f_vtr, reads=[vf_b, ident_b], writes=[pst_b])

                def f_vte(eng, tg=tg, c4=c4, pst=pst):
                    ch = tg * 4 + c4
                    eng.tensor_copy(vt.ap()[:, ch * 132:ch * 132 + 64],
                                    pst.ap().bitcast(BF16)[:, 0:64])
                    return eng.tensor_copy(vt.ap()[:, ch * 132 + 66:ch * 132 + 130],
                                           pst.ap().bitcast(BF16)[:, 64:128])
                P.op("vector", f_vte, reads=[pst_b], writes=[vt_b])

        def sc_op(qb, kc):
            scb = [ctx.ps[2 * (kc % 3)], ctx.ps[2 * (kc % 3) + 1]]
            pbs = [pball[2 * (kc % 4)], pball[2 * (kc % 4) + 1]]

            def f_sc(eng, qb=qb, kc=kc):
                last = None
                for h in range(2):
                    last = eng.matmul(
                        scb[h][0].ap(),
                        kt.ap()[h * 64:h * 64 + 64, kc * 128:kc * 128 + 128],
                        qt.ap()[h * 64:h * 64 + 64, qb * 512:(qb + 1) * 512],
                        start=True, stop=True)
                return last
            P.op("tensor", f_sc, reads=[qt_b, kt_b], writes=[scb[0][1], scb[1][1]])

            def f_exp(eng):
                nb, _ = ctx.get("negB")
                bcol = 0 if is_b0 else 1
                last = None
                for h in range(2):
                    last = eng.activation(out=pbs[h][0].ap(), in_=scb[h][0].ap(),
                                          func=AF.Exp, scale=q_scale,
                                          bias=nb.ap()[:, bcol:bcol + 1])
                return last
            P.op("scalar", f_exp, reads=[scb[0][1], scb[1][1], ctx.get("negB")[1]],
                 writes=[pbs[0][1], pbs[1][1]])

            d0 = 512 * qb - 128 * kc
            vs = None
            if is_b0:
                vs = B0_MASK_MS.index(min(d0, 256)) + (0 if hp < 4 else 6)
            elif d0 <= 0:
                vs = A1_MASK_MS.index(d0)
            if vs is not None:
                def f_msk(eng, vs=vs):
                    last = None
                    with ALP(reason="bf16 mask"):
                        for h in range(2):
                            last = eng.tensor_tensor(
                                pbs[h][0].ap(), pbs[h][0].ap(),
                                msk.ap()[:, vs * 512:(vs + 1) * 512], ALU.mult)
                    return last
                P.op("vector", f_msk, reads=[pbs[0][1], pbs[1][1], msk_b],
                     writes=[pbs[0][1], pbs[1][1]])

        def av_op(qb, kc, n_kc):
            av = [ctx.ps[6], ctx.ps[7]]
            pbs = [pball[2 * (kc % 4)], pball[2 * (kc % 4) + 1]]

            def f_av(eng, kc=kc, n_kc=n_kc, av=tuple(x[0] for x in av)):
                last = None
                for h in range(2):
                    last = eng.matmul(
                        av[h].ap()[0:65, :],
                        vt.ap()[:, kc * 132 + h * 66: kc * 132 + h * 66 + 65],
                        pbs[h][0].ap(),
                        start=(kc == 0), stop=(kc == n_kc - 1))
                return last
            P.op("tensor", f_av, reads=[pbs[0][1], pbs[1][1], vt_b],
                 writes=[av[0][1], av[1][1]])

        for qb in range(NTG):
            n_kc = (qb + 1) * 4
            av = [ctx.ps[6], ctx.ps[7]]
            for kc in range(n_kc):
                sc_op(qb, kc)
                if kc >= 2:
                    av_op(qb, kc - 2, n_kc)
            for kc in (n_kc - 2, n_kc - 1):
                if kc >= 0:
                    av_op(qb, kc, n_kc)

            av2, av2_b = ctx.get("av2")

            def f_evac(eng, av=tuple(x[0] for x in av)):
                eng.tensor_copy(av2.ap()[:, 0:512], av[0].ap())
                return eng.tensor_copy(av2.ap()[:, 512:1024], av[1].ap())
            P.op("vector", f_evac, reads=[av[0][1], av[1][1]], writes=[av2_b])

            psb6, psb6_b = ctx.ps[0]
            psb7, psb7_b = ctx.ps[1]

            def f_bc2(eng):
                eng.matmul(psb6.ap()[0:64, :], e64sel.ap()[64:65, :],
                           av2.ap()[64:65, 0:512], start=True, stop=True)
                return eng.matmul(psb7.ap()[0:64, :], e64sel.ap()[64:65, :],
                                  av2.ap()[64:65, 512:1024], start=True, stop=True)
            P.op("tensor", f_bc2, reads=[av2_b, e64sel_b], writes=[psb6_b, psb7_b])

            hla2, hla2_b = ctx.get("hl_a")
            hlb2, hlb2_b = ctx.get("hl_b")

            def f_rcp2(eng):
                eng.reciprocal(hla2.ap()[0:64, :], psb6.ap()[0:64, :])
                return eng.reciprocal(hlb2.ap()[0:64, :], psb7.ap()[0:64, :])
            P.op("vector", f_rcp2, reads=[psb6_b, psb7_b], writes=[hla2_b, hlb2_b])

            for h in range(2):
                rt2, rt2_b = (hla2, hla2_b) if h == 0 else (hlb2, hlb2_b)

                def f_div(eng, h=h, qb=qb, slot=slot, rt2=rt2):
                    o = ot.ap()[h * 64:h * 64 + 64,
                                slot * T + qb * 512: slot * T + qb * 512 + 512]
                    with ALP(reason="bf16 attn out"):
                        return eng.tensor_tensor(o, av2.ap()[0:64, h * 512:h * 512 + 512],
                                                 rt2.ap()[0:64, :], ALU.mult)
                P.op("vector", f_div, reads=[av2_b, rt2_b], writes=[ot_b])

    def out_proj(hp0, first_half):
        wo, wo_b = ctx.get("wosl")
        ot, ot_b = ctx.get("otile")

        def f_ldwo(e, hp0=hp0):
            src = ctx.dram[f"{kind}_w_o"].ap()[hp0:hp0 + 4]
            return e.dma_start(out=wo.ap().rearrange("p (c d) -> p c d", c=4),
                               in_=src.transpose([1, 0, 2]))
        P.op("sync", f_ldwo, writes=[wo_b], dma=True)
        orb, orb_b = ctx.get("orbuf")
        bo, bo_b = ctx.get("bo_sb")
        ldsrc = src_dram if first_half else dst_dram
        for tg in range(NTG):
            def f_ld(e, tg=tg):
                return e.dma_start(
                    out=orb.ap().rearrange("p (c t) -> p c t", c=NC),
                    in_=ldsrc.ap().rearrange("(c p) t -> p c t", p=128)[:, :, tg * 512:(tg + 1) * 512])
            P.op("sync", f_ld, writes=[orb_b], dma=True)
            for do in range(NC):
                psp, psp_b = ctx.ps[do % 2]

                def f_omm(eng, do=do, tg=tg, psp=psp):
                    last = None
                    for hq in range(4):
                        lhsT = wo.ap()[:, hq * D + do * 128: hq * D + do * 128 + 128]
                        rhs = ot.ap()[:, hq * T + tg * 512: hq * T + tg * 512 + 512]
                        last = eng.matmul(psp.ap(), lhsT, rhs,
                                          start=(hq == 0), stop=(hq == 3))
                    return last
                P.op("tensor", f_omm, reads=[ot_b, wo_b], writes=[psp_b])

                if (not is_b0) and (not first_half):
                    def f_add(eng, do=do, psp=psp):
                        o = orb.ap().rearrange("p (c t) -> p c t", c=NC)[:, do, :]
                        return eng.scalar_tensor_tensor(o, psp.ap(),
                                                        bo.ap()[:, do:do + 1], o,
                                                        ALU.add, ALU.add)
                    P.op("vector", f_add, reads=[psp_b, bo_b, orb_b], writes=[orb_b])
                else:
                    def f_add(eng, do=do, psp=psp):
                        o = orb.ap().rearrange("p (c t) -> p c t", c=NC)[:, do, :]
                        return eng.tensor_add(o, o, psp.ap())
                    P.op("vector", f_add, reads=[psp_b, orb_b], writes=[orb_b])

            def f_st(e, tg=tg):
                return e.dma_start(
                    out=dst_dram.ap().rearrange("(c p) t -> p c t", p=128)[:, :, tg * 512:(tg + 1) * 512],
                    in_=orb.ap().rearrange("p (c t) -> p c t", c=NC))
            P.op("gpsimd", f_st, reads=[orb_b], dma=True)

    for half in range(2):
        for sl in range(4):
            pair_block(half * 4 + sl, sl)
        pbn = [f"pb{i}" for i in range(8)]
        toks = P.barrier()
        ctx.renew(toks, "qt", "kt", "vt", "orbuf", "wpair", "wosl", *pbn)
        out_proj(half * 4, half == 0)
        toks = P.barrier()
        ctx.renew(toks, "qt", "kt", "vt", "orbuf", "wpair", "wosl", "otile",
                  *pbn)



def emit_b1(ctx, src_dram, dst_dram):
    nc, P = ctx.nc, ctx.P
    heads, heads_b = ctx.get("hfull")
    xln_s, xln_s_b = ctx.get("b1_xg")
    sm, _ = ctx.get("b1_sm")
    ck_b, sk_b, cq_b, sq_b, kc_b, ks_b, denr_b, th_b = (Buf(n) for n in
        ("ck", "sk", "cq", "sq", "kc", "ks", "denr", "th"))

    def sm_ap(slot, w=512):
        return sm.ap()[0:16, slot * 512:slot * 512 + w]
    th2, th2_b = ctx.get("b1_th2")
    kc2, _ = ctx.get("b1_kc2")
    vt_s, vt_b = ctx.get("b1_vt")
    scin, scin_b = ctx.get("b1_scin")
    cum, cum_b = ctx.get("b1_cum")
    carry, carry_b = ctx.get("b1_carry")
    zer, zer_b = ctx.get("b1_zer")
    hh, hh_b = ctx.get("b1_hh")
    wqk, wqk_b = ctx.get("b1_wqk")
    wv, wv_b = ctx.get("wslot0")
    wo, wo_b = ctx.get("wslot1")
    sel, sel_b = ctx.get("b1_sel")
    cpio2, cpio2_b = ctx.get("b1_cpio2")
    tmp, tmp_b = ctx.get("lntmp")
    ALP = nc.allow_low_precision

    def f_ldwqk(e):
        return e.dma_start(out=wqk.ap().rearrange("p (c d) -> p c d", c=NC),
                           in_=ctx.dram["b1_wqkd"].ap().transpose([1, 0, 2]))
    P.op("sync", f_ldwqk, writes=[wqk_b], dma=True)
    emit_load_w(ctx, "b1_wvd", "wslot0")
    emit_load_w(ctx, "b1_wod", "wslot1")
    P.op("sync", lambda e: e.dma_start(out=sel.ap()[0:16, :],
                                       in_=ctx.dram["b1_seld"].ap()),
         writes=[sel_b], dma=True)
    P.op("vector", lambda e: e.memset(cpio2.ap(), PI / 2), writes=[cpio2_b])
    P.op("vector", lambda e: e.memset(zer.ap(), 0.0), writes=[zer_b])

    for tg in range(NTG):
        rb = f"rbuf{tg % 2}"
        emit_load_res(ctx, src_dram, tg, rb)
        emit_ln_tg(ctx, rb, "b1_xg")

        def xls(dc):
            return xln_s.ap().rearrange("p (c t) -> p c t", c=NC)[:, dc, :]

        psq, psq_b = ctx.ps[0]
        psk, psk_b = ctx.ps[1]

        def f_qk(eng):
            last = None
            for dc in range(NC):
                last = eng.matmul(psq.ap()[0:16, :], wqk.ap()[:, dc * 32:dc * 32 + 16],
                                  xls(dc), start=(dc == 0), stop=(dc == NC - 1))
            for dc in range(NC):
                last = eng.matmul(psk.ap()[0:16, :], wqk.ap()[:, dc * 32 + 16:dc * 32 + 32],
                                  xls(dc), start=(dc == 0), stop=(dc == NC - 1))
            return last
        P.op("tensor", f_qk, reads=[xln_s_b, wqk_b], writes=[psq_b, psk_b])

        def f_tanh(eng):
            eng.activation(out=sm_ap(7), in_=psq.ap()[0:16, :], func=AF.Tanh)
            return eng.activation(out=th2.ap()[0:16, :], in_=psk.ap()[0:16, :], func=AF.Tanh)
        P.op("scalar", f_tanh, reads=[psq_b, psk_b], writes=[th_b, th2_b])

        def f_sins(eng):
            eng.activation(out=sm_ap(0), in_=th2.ap()[0:16, :], func=AF.Sin,
                           scale=PI / 4, bias=cpio2.ap()[0:16, 0:1])
            eng.activation(out=sm_ap(1), in_=th2.ap()[0:16, :], func=AF.Sin, scale=PI / 4)
            eng.activation(out=sm_ap(2), in_=sm_ap(7), func=AF.Sin,
                           scale=PI / 4, bias=cpio2.ap()[0:16, 0:1])
            return eng.activation(out=sm_ap(3), in_=sm_ap(7), func=AF.Sin, scale=PI / 4)
        P.op("scalar", f_sins, reads=[th_b, th2_b, cpio2_b],
             writes=[ck_b, sk_b, cq_b, sq_b])

        def f_kcum(eng, tg=tg):
            ini_c = 0.0 if tg == 0 else carry.ap()[0:16, 16:17]
            ini_s = 0.0 if tg == 0 else carry.ap()[0:16, 17:18]
            eng.tensor_tensor_scan(kc2.ap()[0:16, 0:512], sm_ap(0), zer.ap()[0:16, 0:512],
                                   ini_c, ALU.add, ALU.add)
            eng.tensor_tensor_scan(kc2.ap()[0:16, 512:1024], sm_ap(1), zer.ap()[0:16, 0:512],
                                   ini_s, ALU.add, ALU.add)
            eng.tensor_copy(carry.ap()[0:16, 16:17], kc2.ap()[0:16, 511:512])
            return eng.tensor_copy(carry.ap()[0:16, 17:18], kc2.ap()[0:16, 1023:1024])
        P.op("vector", f_kcum, reads=[ck_b, sk_b, zer_b, carry_b],
             writes=[kc_b, ks_b, carry_b])

        def f_den(eng):
            eng.tensor_tensor(tmp.ap()[0:16, :], sm_ap(2), kc2.ap()[0:16, 0:512], ALU.mult)
            eng.tensor_tensor(hh.ap()[0:16, 0:512], sm_ap(3), kc2.ap()[0:16, 512:1024], ALU.mult)
            eng.tensor_add(tmp.ap()[0:16, :], tmp.ap()[0:16, :], hh.ap()[0:16, 0:512])
            eng.tensor_scalar_add(tmp.ap()[0:16, :], tmp.ap()[0:16, :], 1e-6)
            with ALP(reason="bf16 reciprocal"):
                return eng.reciprocal(sm_ap(6), tmp.ap()[0:16, :])
        P.op("vector", f_den, reads=[cq_b, sq_b, kc_b, ks_b],
             writes=[denr_b, tmp_b, hh_b])

        for cc in range(NC):
            psv, psv_b = ctx.ps[cc % 2]

            def f_vmm(eng, cc=cc, psv=psv):
                last = None
                for dc in range(NC):
                    lhsT = wv.ap()[:, dc * D + cc * 128: dc * D + cc * 128 + 128]
                    last = eng.matmul(psv.ap(), lhsT, xls(dc),
                                      start=(dc == 0), stop=(dc == NC - 1))
                return last
            P.op("tensor", f_vmm, reads=[xln_s_b, ctx.get("wslot0")[1]], writes=[psv_b])

            def f_vev(eng, psv=psv):
                return eng.tensor_copy(vt_s.ap(), psv.ap())
            P.op("vector", f_vev, reads=[psv_b], writes=[vt_b])

            psc, psc_b = ctx.ps[2]
            pss, pss_b = ctx.ps[3]

            def f_bc(eng, cc=cc):
                eng.matmul(psc.ap(), sel.ap()[0:16, cc * 128:cc * 128 + 128],
                           sm_ap(0), start=True, stop=True)
                return eng.matmul(pss.ap(), sel.ap()[0:16, cc * 128:cc * 128 + 128],
                                  sm_ap(1), start=True, stop=True)
            P.op("tensor", f_bc, reads=[sel_b, ck_b, sk_b], writes=[psc_b, pss_b])

            def f_sc(eng):
                eng.tensor_tensor(scin.ap()[:, 0:512], vt_s.ap(), psc.ap(), ALU.mult)
                return eng.tensor_tensor(scin.ap()[:, 512:1024], vt_s.ap(), pss.ap(), ALU.mult)
            P.op("vector", f_sc, reads=[vt_b, psc_b, pss_b], writes=[scin_b])

            def f_scan(eng, cc=cc, tg=tg):
                ini_c = 0.0 if tg == 0 else carry.ap()[:, 2 * cc:2 * cc + 1]
                ini_s = 0.0 if tg == 0 else carry.ap()[:, 2 * cc + 1:2 * cc + 2]
                eng.tensor_tensor_scan(cum.ap()[:, 0:512], scin.ap()[:, 0:512],
                                       zer.ap()[:, 0:512], ini_c, ALU.add, ALU.add)
                eng.tensor_tensor_scan(cum.ap()[:, 512:1024], scin.ap()[:, 512:1024],
                                       zer.ap()[:, 0:512], ini_s, ALU.add, ALU.add)
                eng.tensor_copy(carry.ap()[:, 2 * cc:2 * cc + 1], cum.ap()[:, 511:512])
                return eng.tensor_copy(carry.ap()[:, 2 * cc + 1:2 * cc + 2], cum.ap()[:, 1023:1024])
            P.op("vector", f_scan, reads=[scin_b, zer_b, carry_b], writes=[cum_b, carry_b])

            psq2, psq2_b = ctx.ps[2]
            pss2, pss2_b = ctx.ps[3]
            psr, psr_b = ctx.ps[4]

            def f_bc2(eng, cc=cc):
                eng.matmul(psq2.ap(), sel.ap()[0:16, cc * 128:cc * 128 + 128],
                           sm_ap(2), start=True, stop=True)
                eng.matmul(pss2.ap(), sel.ap()[0:16, cc * 128:cc * 128 + 128],
                           sm_ap(3), start=True, stop=True)
                return eng.matmul(psr.ap(), sel.ap()[0:16, cc * 128:cc * 128 + 128],
                                  sm_ap(6), start=True, stop=True)
            P.op("tensor", f_bc2, reads=[sel_b, cq_b, sq_b, denr_b],
                 writes=[psq2_b, pss2_b, psr_b])

            def f_h(eng, cc=cc, tg=tg):
                eng.tensor_tensor(hh.ap()[:, 0:512], cum.ap()[:, 0:512], psq2.ap(), ALU.mult)
                eng.tensor_tensor(hh.ap()[:, 512:1024], cum.ap()[:, 512:1024], pss2.ap(), ALU.mult)
                eng.tensor_add(hh.ap()[:, 0:512], hh.ap()[:, 0:512], hh.ap()[:, 512:1024])
                o = heads.ap().rearrange("p (c t) -> p c t", c=NC)[:, cc, tg * 512:(tg + 1) * 512]
                with ALP(reason="bf16 heads"):
                    return eng.tensor_tensor(o, hh.ap()[:, 0:512], psr.ap(), ALU.mult)
            P.op("vector", f_h, reads=[cum_b, psq2_b, pss2_b, psr_b],
                 writes=[hh_b, heads_b])

    for tg in range(NTG):
        rb = f"rbuf{tg % 2}"
        emit_load_res(ctx, src_dram, tg, rb)
        rbuf, rbuf_b = ctx.get(rb)
        for do in range(NC):
            psh, psb = ctx.ps[do % 2]

            def f_omm(eng, do=do, tg=tg, psh=psh):
                last = None
                for cc in range(NC):
                    lhsT = wo.ap()[:, cc * D + do * 128: cc * D + do * 128 + 128]
                    rhs = heads.ap().rearrange("p (c t) -> p c t", c=NC)[:, cc, tg * 512:(tg + 1) * 512]
                    last = eng.matmul(psh.ap(), lhsT, rhs, start=(cc == 0), stop=(cc == NC - 1))
                return last
            P.op("tensor", f_omm, reads=[heads_b, ctx.get("wslot1")[1]], writes=[psb])

            def f_add(eng, do=do, psh=psh, rbuf=rbuf):
                o = rbuf.ap().rearrange("p (c t) -> p c t", c=NC)[:, do, :]
                return eng.tensor_add(o, o, psh.ap())
            P.op("vector", f_add, reads=[psb, rbuf_b], writes=[rbuf_b])
        emit_store_res(ctx, dst_dram, tg, rb)



def _wmat(w):
    import ml_dtypes
    din, dout = w.shape
    return np.ascontiguousarray(w.reshape(NC, 128, dout)).astype(ml_dtypes.bfloat16)


def _wvec(v):
    return np.ascontiguousarray(np.asarray(v, np.float32).reshape(NC, 128).T)



STAGE_NAMES = ["a0", "ffn0a", "b0", "ffn0b", "a1", "ffn1a", "b1", "ffn1b"]

DRAM_SPECS = {
    "common": [("identd", [128, 128], BF16)],
    "ffn0": [("w_ffn0_w", [NC, 128, D], BF16), ("w_ffn0_a1", [NC, 128, D], BF16),
             ("w_ffn0_a2", [NC, 128, D], BF16), ("b_ffn0d", [128, NC], F32)],
    "ffn1": [("w_ffn1_r", [NC, 128, D], BF16), ("w_ffn1_k", [NC, 128, D], BF16),
             ("w_ffn1_v", [NC, 128, D], BF16), ("mu_rd", [128, NC], F32),
             ("mu_kd", [128, NC], F32)],
    "a0": [("w_a0_q", [NC, 128, D], BF16), ("w_a0_k", [NC, 128, D], BF16),
           ("w_a0_v", [NC, 128, D], BF16), ("a0_bqd", [128, NC], F32),
           ("a0_bkd", [128, NC], F32), ("a0_bvrd", [1, D], F32),
           ("a0_gam", [1, 2], F32), ("cmaskTd", [128, 128], BF16)],
    "b0": [("b0_w_q", [NC, 128, D], BF16), ("b0_w_k", [NC, 128, D], BF16),
           ("b0_w_v", [NC, 128, D], BF16), ("b0_w_o", [NC, 128, D], BF16),
           ("msks_b0", [128, 16 * 512], BF16), ("e64seld", [128, 64], F32)],
    "a1": [("a1_w_q", [NC, 128, D], BF16), ("a1_w_k", [NC, 128, D], BF16),
           ("a1_w_v", [NC, 128, D], BF16), ("a1_w_o", [NC, 128, D], BF16),
           ("msks_a1", [128, 16 * 512], BF16), ("e64seld", [128, 64], F32),
           ("e64d", [128, 2], BF16), ("E2d", [2, 128], BF16),
           ("gqbd", [128, 4], F32), ("bod", [128, NC], F32)],
    "b1": [("b1_wqkd", [NC, 128, 32], BF16), ("b1_wvd", [NC, 128, D], BF16),
           ("b1_wod", [NC, 128, D], BF16), ("b1_seld", [16, 8 * 128], BF16)],
}


def build_program(lo, hi):
    import contextlib
    nc = bass.Bass("TRN2", target_bir_lowering=False, debug=False)
    x_in = nc.dram_tensor("x", [D, T], F32, kind="ExternalInput")
    y_out = nc.dram_tensor("y", [D, T], F32, kind="ExternalOutput")
    res = nc.dram_tensor("resid", [D, T], F32, kind="Internal")

    stages = STAGE_NAMES[lo:hi]
    kinds = set(s.rstrip("ab") for s in stages)

    dram = {}
    for k in sorted(kinds | {"common"}):
        for (dn, shape, dt) in DRAM_SPECS.get(k, []):
            if dn not in dram:
                dram[dn] = nc.dram_tensor(dn, shape, dt, kind="ExternalInput")

    with contextlib.ExitStack() as st:
        prog = Prog(nc)
        ctx = Ctx(nc, prog)
        ctx.dram = dram

        slab_w = st.enter_context(nc.sbuf_tensor("slab_w", [128, 8192], F32))
        slab_1 = st.enter_context(nc.sbuf_tensor("slab_1", [128, 16384], F32))
        slab_2 = st.enter_context(nc.sbuf_tensor("slab_2", [128, 16384], F32))
        slab_3 = st.enter_context(nc.sbuf_tensor("slab_3", [128, 6400], F32))
        for i in range(8):
            h = st.enter_context(nc.psum_tensor(f"psb{i}", [128, 512], F32))
            ctx.ps.append((h, Buf(f"psb{i}")))
        lntmp = st.enter_context(nc.sbuf_tensor("lntmp", [128, 512], F32))
        lnv2 = st.enter_context(nc.sbuf_tensor("lnv2", [128, 512], F32))
        onesc = st.enter_context(nc.sbuf_tensor("onesc", [128, 1], BF16))
        onesc64 = st.enter_context(nc.sbuf_tensor("onesc64", [128, 64], BF16))
        onesr = st.enter_context(nc.sbuf_tensor("onesr", [1, 128], F32))
        consts = st.enter_context(nc.sbuf_tensor("consts", [1, 8], F32))
        smalls = st.enter_context(nc.sbuf_tensor("smalls", [128, 256], F32))

        ctx.direct("lntmp", lntmp)
        ctx.direct("lnv2", lnv2)
        ctx.direct("ones_colb", onesc)
        ctx.direct("ones_c64", onesc64)
        ctx.direct("ones_row", onesr)
        ctx.direct("consts", consts)

        ctx.tile("rbuf0", slab_2, F32, 0, NC * 512)
        ctx.tile("rbuf1", slab_2, F32, 4096, NC * 512)
        ctx.tile("lnx2", slab_2, BF16, 8192, NC * 512)
        ctx.tile("lnxb", slab_2, BF16, 10240, NC * 512)
        ctx.tile("xg", slab_3, BF16, 0, NC * 512)
        ctx.tile("xp", slab_3, BF16, 2048, NC * 512)
        ctx.tile("mixg", slab_3, BF16, 4096, NC * 512)
        ctx.tile("tg_out", slab_2, BF16, 12288, NC * 512)
        ctx.tile("carry8", slab_3, BF16, 6144, NC)
        ctx.tile("hfull", slab_1, BF16, 0, NC * T)
        ctx.tile("wslot2", slab_2, BF16, 12288, NC * D)
        ctx.tile("a0_grp", slab_1, BF16, 0, 12288)
        ctx.tile("a0_S", slab_1, BF16, 6144, NC * 1024)
        ctx.tile("a0_tab", slab_1, F32, 10240, 1024)
        ctx.tile("a0_bvbc", slab_1, BF16, 11264, 1024)
        ctx.tile("a0_ktm", slab_1, BF16, 11776, 1024)
        ctx.tile("a0_mdec", slab_1, BF16, 12288, 128)
        ctx.tile("a0_qdecb", slab_1, BF16, 12352, 128)
        ctx.tile("a0_kdec", slab_1, F32, 12416, 2)
        ctx.tile("a0_z", slab_1, F32, 12424, NC)
        ctx.tile("a0_zbf", slab_1, BF16, 12432, NC)
        ctx.tile("a0_cmask", slab_1, BF16, 12504, 128)
        ctx.tile("a0_bvrsb", slab_1, F32, 12568, D)
        ctx.tile("a0_gamsb", smalls, F32, 24, 2, scratch=False)
        ctx.tile("a0_bqsb", smalls, F32, 32, NC, scratch=False)
        ctx.tile("a0_bksb", smalls, F32, 40, NC, scratch=False)
        for tg in range(NTG):
            ctx.tile(f"xln{tg}", slab_1, BF16, tg * 2048, NC * 512)
        ctx.tile("qt", slab_3, BF16, 0, 4096)
        ctx.tile("kt", slab_3, BF16, 2048, 4096)
        ctx.tile("vt", slab_3, BF16, 4096, 32 * 132)
        ctx.tile("av2", slab_2, F32, 14592, 1024)
        ctx.tile("orbuf", slab_3, F32, 0, NC * 512)
        ctx.tile("otile", slab_2, BF16, 0, 4 * T)
        ctx.tile("pb0", slab_2, BF16, 12288, 512)
        ctx.tile("pb1", slab_2, BF16, 12544, 512)
        ctx.tile("pb2", slab_2, BF16, 15616, 512)
        ctx.tile("pb3", slab_2, BF16, 15872, 512)
        ctx.tile("pb4", slab_2, BF16, 13824, 512)
        ctx.tile("pb5", slab_2, BF16, 14080, 512)
        ctx.tile("pb6", slab_2, BF16, 14336, 512)
        ctx.tile("pb7", slab_2, BF16, 16128, 512)
        ctx.tile("hl_a", slab_2, F32, 12800, 512)
        ctx.tile("hl_b", slab_2, F32, 13312, 512)
        ctx.tile("hl_sq", slab_2, BF16, 13824, 512)
        ctx.tile("hl_gb", slab_2, BF16, 14080, 1024)
        ctx.tile("wpair", slab_w, BF16, 0, 3 * NC * 128)
        ctx.tile("wosl", slab_w, BF16, 0, 4 * D)
        ctx.tile("mskt", slab_w, BF16, 4096, 16 * 512)
        ctx.tile("b1_sm", slab_3, BF16, 0, 8 * 512)
        ctx.tile("b1_th2", slab_3, BF16, 2048, 512)
        ctx.tile("b1_kc2", slab_3, F32, 2304, 1024)
        ctx.tile("b1_vt", slab_3, BF16, 3328, 512)
        ctx.tile("b1_scin", slab_3, BF16, 3584, 1024)
        ctx.tile("b1_cum", slab_3, F32, 4096, 1024)
        ctx.tile("b1_carry", slab_3, F32, 5120, 20)
        ctx.tile("b1_zer", slab_3, BF16, 5140, 512)
        ctx.tile("b1_wqk", slab_3, BF16, 5396, NC * 32)
        ctx.tile("b1_cpio2", slab_3, F32, 5524, 1)
        ctx.tile("b1_sel", slab_3, BF16, 5526, 8 * 128)
        ctx.tile("b1_hh", slab_2, F32, 12800, 1024)
        ctx.tile("b1_xg", slab_2, BF16, 13824, NC * 512)
        ctx.tile("gqb", smalls, F32, 48, 4, scratch=False)
        ctx.tile("ceps", smalls, F32, 52, 1, scratch=False)
        ctx.tile("bo_sb", smalls, F32, 53, NC, scratch=False)
        ctx.tile("e64", smalls, BF16, 61, 2, scratch=False)
        ctx.tile("e64sel", smalls, F32, 62, 64, scratch=False)
        ctx.tile("E2t", smalls, BF16, 126, 128, scratch=False)
        ctx.tile("ident", smalls, BF16, 190, 128, scratch=False)
        ctx.tile("negB", smalls, F32, 254, 2, scratch=False)
        ctx.tile("wslot0", slab_w, BF16, 0, NC * D)
        ctx.tile("wslot1", slab_w, BF16, 4096, NC * D)
        ctx.tile("b_ffn0", smalls, F32, 0, NC, scratch=False)
        ctx.tile("v_mu_r", smalls, F32, 8, NC, scratch=False)
        ctx.tile("v_mu_k", smalls, F32, 16, NC, scratch=False)

        P = prog
        P.op("vector", lambda e: e.memset(onesc.ap(), 1.0),
             writes=[ctx.get("ones_colb")[1]])
        P.op("vector", lambda e: e.memset(onesc64.ap(), 1.0),
             writes=[ctx.get("ones_c64")[1]])
        P.op("vector", lambda e: e.memset(onesr.ap(), 1.0),
             writes=[ctx.get("ones_row")[1]])

        def f_csts(e):
            e.memset(consts.ap()[0:1, 0:1], 1e-5)
            return e.memset(consts.ap()[0:1, 1:2], 1e-6)
        P.op("vector", f_csts, writes=[ctx.get("consts")[1]])
        P.op("vector", lambda e: e.memset(lntmp.ap(), 0.0),
             writes=[ctx.get("lntmp")[1]])
        P.op("vector", lambda e: e.memset(ctx.get("ceps")[0].ap(), 1e-5),
             writes=[ctx.get("ceps")[1]])

        def f_negb(e):
            e.memset(ctx.get("negB")[0].ap()[:, 0:1], -EXPB_B0)
            return e.memset(ctx.get("negB")[0].ap()[:, 1:2], -EXPB_A1)
        P.op("vector", f_negb, writes=[ctx.get("negB")[1]])
        P.op("sync", lambda e: e.dma_start(out=ctx.get("ident")[0].ap(),
                                           in_=dram["identd"].ap()),
             writes=[ctx.get("ident")[1]], dma=True)
        if kinds & {"b0", "a1"}:
            P.op("sync", lambda e: e.dma_start(out=ctx.get("e64sel")[0].ap(),
                                               in_=dram["e64seld"].ap()),
                 writes=[ctx.get("e64sel")[1]], dma=True)
        if "a1" in kinds:
            P.op("sync", lambda e: e.dma_start(out=ctx.get("e64")[0].ap(),
                                               in_=dram["e64d"].ap()),
                 writes=[ctx.get("e64")[1]], dma=True)
            P.op("sync", lambda e: e.dma_start(out=ctx.get("E2t")[0].ap()[0:2, :],
                                               in_=dram["E2d"].ap()),
                 writes=[ctx.get("E2t")[1]], dma=True)
            P.op("sync", lambda e: e.dma_start(out=ctx.get("gqb")[0].ap(),
                                               in_=dram["gqbd"].ap()),
                 writes=[ctx.get("gqb")[1]], dma=True)
            P.op("sync", lambda e: e.dma_start(out=ctx.get("bo_sb")[0].ap(),
                                               in_=dram["bod"].ap()),
                 writes=[ctx.get("bo_sb")[1]], dma=True)

        if "ffn0" in kinds:
            def f_ldb(e):
                return e.dma_start(out=ctx.get("b_ffn0")[0].ap(),
                                   in_=dram["b_ffn0d"].ap())
            P.op("sync", f_ldb, writes=[ctx.get("b_ffn0")[1]], dma=True)
        if "ffn1" in kinds:
            P.op("sync", lambda e: e.dma_start(out=ctx.get("v_mu_r")[0].ap(),
                                               in_=dram["mu_rd"].ap()),
                 writes=[ctx.get("v_mu_r")[1]], dma=True)
            P.op("sync", lambda e: e.dma_start(out=ctx.get("v_mu_k")[0].ap(),
                                               in_=dram["mu_kd"].ap()),
                 writes=[ctx.get("v_mu_k")[1]], dma=True)

        cur_src = x_in
        for si, sname in enumerate(stages):
            last = si == len(stages) - 1
            dst = y_out if last else res
            kind = sname.rstrip("ab")
            if kind == "ffn0":
                emit_ffn0(ctx, cur_src, dst)
            elif kind == "ffn1":
                emit_ffn1(ctx, cur_src, dst)
            elif kind == "a0":
                emit_a0(ctx, cur_src, dst)
            elif kind in ("b0", "a1"):
                emit_mha(ctx, cur_src, dst, kind)
            elif kind == "b1":
                emit_b1(ctx, cur_src, dst)
            else:
                raise NotImplementedError(sname)
            cur_src = res if not last else None
            if not last:
                toks = P.barrier()
                ctx.renew(toks, *ctx.scratch)

        P.op("gpsimd", lambda e: e.engine_nop(),
             after=[("gD", prog.plan["gD"], "gpsimd", True)])

        build_and_finalize(nc, prog)
    return nc


_PROG_CACHE = {}

DBG_RANGE = (0, 8)
DBG_X0 = None


def _staged_inputs(inputs):
    g = lambda n: np.asarray(inputs[n], np.float32)
    d = {}
    d["w_ffn0_w"] = _wmat(g("L0_ffn_weight"))
    d["w_ffn0_a1"] = _wmat(g("L0_ffn_aux_weight1"))
    d["w_ffn0_a2"] = _wmat(g("L0_ffn_aux_weight2"))
    d["b_ffn0d"] = _wvec(g("L0_ffn_bias"))
    d["w_ffn1_r"] = _wmat(g("L1_ffn_W_r").T)
    d["w_ffn1_k"] = _wmat(g("L1_ffn_W_k").T)
    d["w_ffn1_v"] = _wmat(g("L1_ffn_W_v").T)
    d["mu_rd"] = _wvec(g("L1_ffn_mu_r"))
    d["mu_kd"] = _wvec(g("L1_ffn_mu_k"))
    import ml_dtypes
    r128 = np.arange(128)
    d["w_a0_q"] = _wmat(g("L0_attn_a_Wq").T)
    d["w_a0_k"] = _wmat(g("L0_attn_a_Wk").T)
    d["w_a0_v"] = _wmat(g("L0_attn_a_Wv").T)
    d["a0_bqd"] = _wvec(g("L0_attn_a_bq"))
    d["a0_bkd"] = _wvec(g("L0_attn_a_bk"))
    d["a0_bvrd"] = np.ascontiguousarray(g("L0_attn_a_bv").reshape(1, D))
    gamma = float(np.asarray(inputs["L0_attn_a_gamma"]))
    d["a0_gam"] = np.array([[gamma, 0.0]], np.float32)
    d["identd"] = np.eye(128, dtype=ml_dtypes.bfloat16)
    d["cmaskTd"] = (r128[None, :] >= r128[:, None]).astype(ml_dtypes.bfloat16)
    d["b0_w_q"] = _wmat(g("L0_attn_b_Wq").T)
    d["b0_w_k"] = _wmat(g("L0_attn_b_Wk").T)
    d["b0_w_v"] = _wmat(g("L0_attn_b_Wv").T)
    d["b0_w_o"] = _wmat(g("L0_attn_b_Wo").T)
    d["a1_w_q"] = _wmat(g("L1_attn_a_Wq").T)
    d["a1_w_k"] = _wmat(g("L1_attn_a_Wk").T)
    d["a1_w_v"] = _wmat(g("L1_attn_a_Wv").T)
    d["a1_w_o"] = _wmat(g("L1_attn_a_Wo").T)
    gq, bq1 = g("L1_attn_a_gq"), g("L1_attn_a_bq")
    gk, bk1 = g("L1_attn_a_gk"), g("L1_attn_a_bk")
    d["gqbd"] = np.stack([np.tile(gq, 2), np.tile(bq1, 2),
                          np.tile(gk, 2), np.tile(bk1, 2)], axis=1).astype(np.float32)
    d["bod"] = _wvec(g("L1_attn_a_bo"))
    e64 = np.zeros((128, 2), np.float32)
    e64[0:64, 0] = 1; e64[64:128, 1] = 1
    d["e64d"] = e64.astype(ml_dtypes.bfloat16)
    E2 = np.zeros((2, 128), np.float32)
    E2[0, 0:64] = 1; E2[1, 64:128] = 1
    d["E2d"] = E2.astype(ml_dtypes.bfloat16)
    e64s = np.zeros((128, 64), np.float32)
    e64s[64, :] = 1.0
    d["e64seld"] = e64s
    r = np.arange(128)[:, None]
    c = np.arange(512)[None, :]
    mb0 = np.zeros((128, 16 * 512), np.float32)
    for i, m in enumerate(B0_MASK_MS):
        delta = m + c - r
        causal = delta >= 0
        mb0[:, i * 512:(i + 1) * 512] = causal & ((delta < 64) | (delta % 64 == 0))
        sameblock = (r // 64) == ((m + c) // 64)
        mb0[:, (6 + i) * 512:(7 + i) * 512] = causal & (sameblock | (r % 64 == 63))
    d["msks_b0"] = mb0.astype(ml_dtypes.bfloat16)
    ma1 = np.zeros((128, 16 * 512), np.float32)
    for i, m in enumerate(A1_MASK_MS):
        ma1[:, i * 512:(i + 1) * 512] = (m + c - r) >= 0
    d["msks_a1"] = ma1.astype(ml_dtypes.bfloat16)
    WQ2 = g("L1_attn_b_W_Q")[:, :, 0].T
    WK2 = g("L1_attn_b_W_K")[:, :, 0].T
    WV2 = np.ascontiguousarray(g("L1_attn_b_W_V").transpose(1, 0, 2).reshape(D, D))
    WO2 = np.ascontiguousarray(g("L1_attn_b_W_O").transpose(2, 1, 0).reshape(D, D))
    d["b1_wqkd"] = _wmat(np.concatenate([WQ2, WK2], axis=1))
    d["b1_wvd"] = _wmat(WV2)
    d["b1_wod"] = _wmat(WO2)
    selmat = np.zeros((16, 8 * 128), np.float32)
    for cc in range(8):
        for p in range(128):
            selmat[2 * cc + (p >= 64), cc * 128 + p] = 1.0
    d["b1_seld"] = selmat.astype(ml_dtypes.bfloat16)
    return d


def kernel(**inputs):
    _ = inputs.get("L1_attn_a_mask")
    lo, hi = DBG_RANGE
    key = (lo, hi)
    if key not in _PROG_CACHE:
        _PROG_CACHE[key] = build_program(lo, hi)
    nc = _PROG_CACHE[key]
    wd = _staged_inputs(inputs)
    x = np.asarray(inputs["L0_attn_a_x_seq"], np.float32)
    if DBG_X0 is not None:
        x = np.asarray(DBG_X0, np.float32)
    stages = STAGE_NAMES[lo:hi]
    kinds = set(s.rstrip("ab") for s in stages)
    need = set()
    for k in kinds | {"common"}:
        for (dn, _s, _d) in DRAM_SPECS.get(k, []):
            need.add(dn)
    in_maps = []
    for c in range(8):
        b = c % 4
        m = {"x": np.ascontiguousarray(x[b].T)}
        for dn in need:
            m[dn] = wd[dn]
        in_maps.append(m)
    res = run_bass_kernel_spmd(nc, in_maps, core_ids=list(range(8)))
    out = np.zeros((B, T, D), np.float32)
    for b in range(4):
        out[b] = res.results[b]["y"].T
    return out
